# version 07
# baseline (speedup 1.0000x reference)
import numpy as np
import ml_dtypes
from contextlib import ExitStack

import concourse.bass as bass
import concourse.mybir as mybir
from concourse.bass_utils import run_bass_kernel_spmd

F32 = mybir.dt.float32
BF16 = mybir.dt.bfloat16
I32 = mybir.dt.int32
AF = mybir.ActivationFunctionType
ALU = mybir.AluOpType
AX = mybir.AxisListType

P = 128
NT = 16
TOK = 2048
D = 1024
DC = 8
H0, H0C = 3072, 24
H1, H1C = 2816, 22
R = 32
NCORE = 8
QG_GROUPS = [(0, 3), (384, 3), (768, 3), (1152, 3), (1536, 2), (1792, 2)]
EPS = 1e-5
INV127 = 1.0 / 127.0
INV127SQ = 1.0 / 16129.0
ENGS = ("sync", "scalar", "vector", "tensor", "gpsimd")


class Ctr:
    def __init__(self, handle, name):
        self.h = handle
        self.name = name
        self.v = 0


class Plan:
    def __init__(self, nc):
        self.nc = nc
        self.ops = {k: [] for k in ENGS}
        self.meta = {k: [] for k in ENGS}

    def wait(self, eng, ctr, tgt):
        if tgt <= 0:
            return
        self.ops[eng].append(lambda E, h=ctr.h, t=tgt: E.wait_ge(h, t))
        self.meta[eng].append(("wait", ctr.name, tgt))

    def _inc(self, eng, ctr, n):
        ctr.v += n
        self.meta[eng].append(("inc", ctr.name, ctr.v))
        return ctr.v

    def dma(self, eng, ctr, out, in_, **kw):
        self.ops[eng].append(
            lambda E, o=out, i=in_, k=kw, h=ctr.h: E.dma_start(out=o, in_=i, **k).then_inc(h, 16))
        return self._inc(eng, ctr, 16)

    def dmat(self, eng, ctr, out, in_):
        self.ops[eng].append(
            lambda E, o=out, i=in_, h=ctr.h: E.dma_start_transpose(o, i).then_inc(h, 16))
        return self._inc(eng, ctr, 16)

    def op(self, eng, ctr, fn):
        self.ops[eng].append(lambda E, f=fn, h=ctr.h: f(E).then_inc(h, 1))
        return self._inc(eng, ctr, 1)

    def eng_inc(self, eng, ctr):
        self.ops[eng].append(lambda E, h=ctr.h: E.sem_inc(h, 1))
        return self._inc(eng, ctr, 1)

    def check(self):
        n_ops = {e: len(self.meta[e]) for e in ENGS}
        done = {e: 0 for e in ENGS}
        sem = {}
        progress, ran, total = True, 0, sum(n_ops.values())
        while progress:
            progress = False
            for e in ENGS:
                while done[e] < n_ops[e]:
                    m = self.meta[e][done[e]]
                    if m[0] == "wait":
                        if sem.get(m[1], 0) < m[2]:
                            break
                    else:
                        sem[m[1]] = max(sem.get(m[1], 0), m[2])
                    done[e] += 1
                    ran += 1
                    progress = True
        if ran != total:
            stuck = {e: (done[e], n_ops[e], self.meta[e][done[e]] if done[e] < n_ops[e] else None)
                     for e in ENGS if done[e] < n_ops[e]}
            raise RuntimeError(f"plan deadlock: ran {ran}/{total}; stuck at {stuck}")


def build_program(dump=False):
    nc = bass.Bass("TRN2", target_bir_lowering=False, debug=False, detect_race_conditions=False)
    ap = lambda t: t.ap()

    x_in = ap(nc.dram_tensor("x", [TOK, D], F32, kind="ExternalInput"))
    w0g_in = ap(nc.dram_tensor("w0g", [H0C, P, D], I32, kind="ExternalInput"))
    w0u_in = ap(nc.dram_tensor("w0u", [H0C, P, D], I32, kind="ExternalInput"))
    w0d_in = ap(nc.dram_tensor("w0d", [H0C, P, D], I32, kind="ExternalInput"))
    w1g_in = ap(nc.dram_tensor("w1g", [H1, D], F32, kind="ExternalInput"))
    w1u_in = ap(nc.dram_tensor("w1u", [H1, D], F32, kind="ExternalInput"))
    w1d_in = ap(nc.dram_tensor("w1d", [D, H1], F32, kind="ExternalInput"))
    u_in = ap(nc.dram_tensor("uproj", [DC, P, R], F32, kind="ExternalInput"))
    clip_in = ap(nc.dram_tensor("clip2", [1, 2], F32, kind="ExternalInput"))
    trilT_in = ap(nc.dram_tensor("trilT", [P, P], BF16, kind="ExternalInput"))
    sel16_in = ap(nc.dram_tensor("sel16", [P, NT * NT], BF16, kind="ExternalInput"))
    maskLT_in = ap(nc.dram_tensor("maskLT", [NT, NT], BF16, kind="ExternalInput"))
    cidm_in = ap(nc.dram_tensor("cidm", [NCORE, 32], BF16, kind="ExternalInput"))
    invcnt_in = ap(nc.dram_tensor("invcnt", [P, NT], F32, kind="ExternalInput"))
    ones_in = ap(nc.dram_tensor("onespack", [P, P], BF16, kind="ExternalInput"))
    onesf_in = ap(nc.dram_tensor("onesf", [1, P], F32, kind="ExternalInput"))

    y_out = ap(nc.dram_tensor("y", [TOK, D], F32, kind="ExternalOutput"))

    d_bs_in = nc.dram_tensor("d_bs_in", [1, D], BF16)
    d_bs_out = nc.dram_tensor("d_bs_out", [NCORE, D], BF16, addr_space="Shared")
    d_v_in = nc.dram_tensor("d_v_in", [TOK, D], BF16)
    d_vgat = nc.dram_tensor("d_vgat", [NCORE * TOK, D], BF16, addr_space="Shared")
    d_y_in = nc.dram_tensor("d_y_in", [R + 2, TOK], BF16)
    d_y_out = nc.dram_tensor("d_y_out", [NCORE * (R + 2), TOK], BF16, addr_space="Shared")
    d_mx1_in = nc.dram_tensor("d_mx1_in", [1, 8], F32)
    d_mx1_out = nc.dram_tensor("d_mx1_out", [1, 8], F32)
    d_mx2_in = nc.dram_tensor("d_mx2_in", [1, 8], F32)
    d_mx2_out = nc.dram_tensor("d_mx2_out", [1, 8], F32)
    d_x3 = ap(nc.dram_tensor("d_x3", [TOK, D], F32))
    d_wgq = ap(nc.dram_tensor("d_wgq", [H1C, P, D], BF16))
    d_wuq = ap(nc.dram_tensor("d_wuq", [H1C, P, D], BF16))
    d_wdq = ap(nc.dram_tensor("d_wdq", [H1C, P, D], BF16))
    if dump:
        tp_x1 = ap(nc.dram_tensor("tp_x1", [TOK, D], F32, kind="ExternalOutput"))
        tp_x2 = ap(nc.dram_tensor("tp_x2", [TOK, D], F32, kind="ExternalOutput"))
        tp_v = ap(nc.dram_tensor("tp_v", [TOK, D], F32, kind="ExternalOutput"))
        tp_y = ap(nc.dram_tensor("tp_y", [R + 2, TOK], F32, kind="ExternalOutput"))
        tp_x3 = ap(nc.dram_tensor("tp_x3", [TOK, D], F32, kind="ExternalOutput"))
        tp_sc = ap(nc.dram_tensor("tp_sc", [1, 16], F32, kind="ExternalOutput"))
        tp_ln0 = ap(nc.dram_tensor("tp_ln0", [TOK, D], F32, kind="ExternalOutput"))
        tp_cc = ap(nc.dram_tensor("tp_cc", [2 * NT + 2, D], F32, kind="ExternalOutput"))

    pl = Plan(nc)

    with ExitStack() as ctx:
        ent = ctx.enter_context

        def sem(name):
            return Ctr(ent(nc.semaphore(name)), name)

        tS, vS, sS, gS = sem("tS"), sem("vS"), sem("sS"), sem("gS")
        qX0 = sem("qX0")
        qX0B = sem("qX0B")
        qCN = sem("qCN")
        qBS1, qBS2 = sem("qBS1"), sem("qBS2")
        qCF = sem("qCF")
        qWG0, qWU0, qWD0 = sem("qWG0"), sem("qWU0"), sem("qWD0")
        qT1 = sem("qT1")
        qTV = sem("qTV")
        qVS = sem("qVS")
        qYS = sem("qYS")
        qYL = sem("qYL")
        qNQ = sem("qNQ")
        qVL = sem("qVL")
        qWJ = sem("qWJ")
        qWT = sem("qWT")
        qSP = sem("qSP")
        qX3W = sem("qX3W")
        qX3L = sem("qX3L")
        qX3D = sem("qX3D")
        qMW, qHW = sem("qMW"), sem("qHW")
        qM1a, qM1b, qM2a, qM2b = sem("qM1a"), sem("qM1b"), sem("qM2a"), sem("qM2b")
        qWQL = sem("qWQL")
        qWDL = sem("qWDL")
        qTX = sem("qTX")
        qYO = sem("qYO")
        cBS, cV, cY, cM1, cM2 = sem("cBS"), sem("cV"), sem("cY"), sem("cM1"), sem("cM2")
        if dump:
            qDP = sem("qDP")

        c_trilT = ent(nc.sbuf_tensor("c_trilT", [P, P], BF16))
        c_sel16 = ent(nc.sbuf_tensor("c_sel16", [P, NT * NT], BF16))
        c_maskLT = ent(nc.sbuf_tensor("c_maskLT", [NT, NT], BF16))
        c_cidm = ent(nc.sbuf_tensor("c_cidm", [NCORE, 32], BF16))
        c_invcnt = ent(nc.sbuf_tensor("c_invcnt", [P, NT], F32))
        c_ones = ent(nc.sbuf_tensor("c_ones", [P, P], BF16))
        c_onesf = ent(nc.sbuf_tensor("c_onesf", [1, P], F32))
        sq_f32 = ent(nc.sbuf_tensor("sq_f32", [P, 2, D], F32))
        st_sq = ent(nc.sbuf_tensor("st_sq", [P, 16], F32))
        st_w = ent(nc.sbuf_tensor("st_w", [P, 7, 16], F32))
        st_sum = ent(nc.sbuf_tensor("st_sum", [P, 16], F32))
        sclip = ent(nc.sbuf_tensor("sclip", [1, 2], F32))
        uU = ent(nc.sbuf_tensor("uU", [P, DC, R], F32))
        uUb = ent(nc.sbuf_tensor("uUb", [P, DC, R], BF16))
        sxb = ent(nc.sbuf_tensor("sxb", [P, 16], F32))
        shb = ent(nc.sbuf_tensor("shb", [P, 16], F32))
        mx_sb = ent(nc.sbuf_tensor("mx_sb", [1, 16], F32))
        mxh_sb = ent(nc.sbuf_tensor("mxh_sb", [1, 16], F32))
        scal_sb = ent(nc.sbuf_tensor("scal_sb", [1, 16], F32))

        s123 = ExitStack()
        xres = s123.enter_context(nc.sbuf_tensor("xres", [P, NT, D], F32))

        t_x0h = pl.dma("sync", qX0, xres[:, 0:8, :], x_in[0:8 * P, :].rearrange("(m p) d -> p m d", p=P))
        t_x0 = pl.dma("sync", qX0B, xres[:, 8:16, :], x_in[8 * P:16 * P, :].rearrange("(m p) d -> p m d", p=P))
        pl.dma("scalar", qCN, c_trilT[:], trilT_in)
        pl.dma("scalar", qCN, c_sel16[:], sel16_in)
        pl.dma("scalar", qCN, c_maskLT[:], maskLT_in)
        pl.dma("scalar", qCN, c_cidm[:], cidm_in)
        pl.dma("scalar", qCN, c_invcnt[:], invcnt_in)
        pl.dma("scalar", qCN, c_ones[:], ones_in)
        pl.dma("scalar", qCN, c_onesf[:], onesf_in)
        pl.dma("scalar", qCN, sclip[:], clip_in)
        t_consts = pl.dma("scalar", qCN, uU[:], u_in.rearrange("c p r -> p c r"))

        def emit_ln_batch(tiles, dst_pre=None, post_dst=None):
            n = len(tiles)
            t_stats = []
            for k, (s, _d, vw) in enumerate(tiles):
                for ctr, tgt in vw:
                    pl.wait("scalar", ctr, tgt)
                if k >= 2:
                    pl.wait("scalar", vS, t_stats[k - 2])
                t_sq2 = pl.op("scalar", sS, lambda E, s=s, k=k: E.activation(
                    sq_f32[:, k % 2, :], s, AF.Square))
                for ctr, tgt in vw:
                    pl.wait("vector", ctr, tgt)
                pl.op("vector", vS, lambda E, s=s, k=k: E.reduce_sum(
                    st_sum[:, k:k + 1], s, axis=AX.X))
                pl.wait("vector", sS, t_sq2)
                t_stats.append(pl.op("vector", vS, lambda E, k=k: E.reduce_sum(
                    st_sq[:, k:k + 1], sq_f32[:, k % 2, :], axis=AX.X)))
            pl.wait("gpsimd", vS, t_stats[-1])
            pl.op("gpsimd", gS, lambda E, n=n: E.tensor_scalar(
                st_w[:, 0, 0:n], st_sum[:, 0:n], 1.0 / D, None, ALU.mult))
            pl.op("gpsimd", gS, lambda E, n=n: E.tensor_scalar(
                st_w[:, 1, 0:n], st_sq[:, 0:n], 1.0 / D, None, ALU.mult))
            pl.op("gpsimd", gS, lambda E, n=n: E.tensor_tensor(
                st_w[:, 2, 0:n], st_w[:, 0, 0:n], st_w[:, 0, 0:n], ALU.mult))
            pl.op("gpsimd", gS, lambda E, n=n: E.tensor_tensor(
                st_w[:, 3, 0:n], st_w[:, 1, 0:n], st_w[:, 2, 0:n], ALU.subtract))
            t_gv = pl.op("gpsimd", gS, lambda E, n=n: E.tensor_scalar(
                st_w[:, 4, 0:n], st_w[:, 3, 0:n], EPS, None, ALU.add))
            pl.wait("scalar", gS, t_gv)
            t_sd = pl.op("scalar", sS, lambda E, n=n: E.activation(
                st_w[:, 5, 0:n], st_w[:, 4, 0:n], AF.Sqrt))
            pl.wait("vector", sS, t_sd)
            pl.op("vector", vS, lambda E, n=n: E.reciprocal(st_w[:, 6, 0:n], st_w[:, 5, 0:n]))
            t_dsts = []
            for k, (s, d, _vw) in enumerate(tiles):
                if dst_pre is not None:
                    dst_pre(k)
                t_dst = pl.op("vector", vS, lambda E, s=s, d=d, k=k: E.tensor_scalar(
                    d, s, st_w[:, 0, k:k + 1], st_w[:, 6, k:k + 1], ALU.subtract, ALU.mult))
                t_dsts.append(t_dst)
                if post_dst is not None:
                    post_dst(k, t_dst)
            return t_dsts, t_stats

        t_prefix_evac = []
        with ExitStack() as ph:
            pe = ph.enter_context
            ln0 = pe(nc.sbuf_tensor("ln0", [P, NT, D], BF16))
            colsums = pe(nc.sbuf_tensor("colsums", [NT, D], BF16))
            bs_row = pe(nc.sbuf_tensor("bs_row", [1, D], BF16))
            gath_bs = pe(nc.sbuf_tensor("gath_bs", [NCORE, D], BF16))
            carry0 = pe(nc.sbuf_tensor("carry0", [1, D], BF16))
            carries = pe(nc.sbuf_tensor("carries", [NT, D], BF16))
            carries_f = pe(nc.sbuf_tensor("carries_f", [1, NT, D], BF16))

            t_ln0, _ = emit_ln_batch(
                [(xres[:, m, :], ln0[:, m, :],
                  [(qX0, t_x0h)] if m == 0 else ([(qX0B, t_x0)] if m == 8 else []))
                 for m in range(NT)])

            with ExitStack() as pp:
                p_cs0 = pp.enter_context(nc.psum_tensor("p_cs0", [NT, 512], F32))
                p_cs1 = pp.enter_context(nc.psum_tensor("p_cs1", [NT, 512], F32))
                pl.wait("tensor", qCN, t_consts)
                for m in range(NT):
                    pl.wait("tensor", vS, t_ln0[m])
                    pl.op("tensor", tS, lambda E, m=m: E.matmul(
                        p_cs0[:, :], c_sel16[:, NT * m:NT * (m + 1)], ln0[:, m, 0:512],
                        start=(m == 0), stop=(m == NT - 1)))
                    t_csm = pl.op("tensor", tS, lambda E, m=m: E.matmul(
                        p_cs1[:, :], c_sel16[:, NT * m:NT * (m + 1)], ln0[:, m, 512:1024],
                        start=(m == 0), stop=(m == NT - 1)))
                pl.wait("vector", tS, t_csm)
                pl.op("vector", vS, lambda E: E.tensor_copy(colsums[:, 0:512], p_cs0[:, :]))
                t_colsums = pl.op("vector", vS, lambda E: E.tensor_copy(colsums[:, 512:1024], p_cs1[:, :]))

            with ExitStack() as pp:
                p_bs0 = pp.enter_context(nc.psum_tensor("p_bs0", [1, 512], F32))
                p_bs1 = pp.enter_context(nc.psum_tensor("p_bs1", [1, 512], F32))
                p_c0a = pp.enter_context(nc.psum_tensor("p_c0a", [1, 512], F32))
                p_c0b = pp.enter_context(nc.psum_tensor("p_c0b", [1, 512], F32))
                pl.wait("tensor", vS, t_colsums)
                pl.op("tensor", tS, lambda E: E.matmul(
                    p_bs0[:, :], c_ones[0:NT, 0:1], colsums[:, 0:512], start=True, stop=True))
                t_bsm = pl.op("tensor", tS, lambda E: E.matmul(
                    p_bs1[:, :], c_ones[0:NT, 0:1], colsums[:, 512:1024], start=True, stop=True))
                pl.wait("vector", tS, t_bsm)
                pl.op("vector", vS, lambda E: E.tensor_copy(bs_row[:, 0:512], p_bs0[:, :]))
                t_bsrow = pl.op("vector", vS, lambda E: E.tensor_copy(bs_row[:, 512:1024], p_bs1[:, :]))
                pl.wait("gpsimd", vS, t_bsrow)
                t_bst = pl.dma("gpsimd", qBS1, d_bs_in.ap(), bs_row[:])
                pl.wait("gpsimd", qBS1, t_bst)
                t_ag_bs = pl.op("gpsimd", cBS, lambda E: E.collective_compute(
                    "AllGather", ALU.bypass, replica_groups=[list(range(NCORE))],
                    ins=[d_bs_in.ap().opt()], outs=[d_bs_out.ap().opt()]))
                pl.wait("gpsimd", cBS, t_ag_bs)
                t_gbs = pl.dma("gpsimd", qBS2, gath_bs[:], d_bs_out.ap())
                pl.wait("tensor", qBS2, t_gbs)
                pl.op("tensor", tS, lambda E: E.matmul(
                    p_c0a[:, :], c_cidm[:, 0:1], gath_bs[:, 0:512], start=True, stop=True))
                t_c0m = pl.op("tensor", tS, lambda E: E.matmul(
                    p_c0b[:, :], c_cidm[:, 0:1], gath_bs[:, 512:1024], start=True, stop=True))
                pl.wait("vector", tS, t_c0m)
                pl.op("vector", vS, lambda E: E.tensor_copy(carry0[:, 0:512], p_c0a[:, :]))
                t_c0 = pl.op("vector", vS, lambda E: E.tensor_copy(carry0[:, 512:1024], p_c0b[:, :]))

            with ExitStack() as pp:
                p_ca0 = pp.enter_context(nc.psum_tensor("p_ca0", [NT, 512], F32))
                p_ca1 = pp.enter_context(nc.psum_tensor("p_ca1", [NT, 512], F32))
                pl.wait("tensor", vS, t_c0)
                pl.op("tensor", tS, lambda E: E.matmul(
                    p_ca0[:, :], c_maskLT[:], colsums[:, 0:512], start=True, stop=False))
                pl.op("tensor", tS, lambda E: E.matmul(
                    p_ca0[:, :], c_ones[0:1, 0:NT], carry0[:, 0:512], start=False, stop=True))
                pl.op("tensor", tS, lambda E: E.matmul(
                    p_ca1[:, :], c_maskLT[:], colsums[:, 512:1024], start=True, stop=False))
                t_cam = pl.op("tensor", tS, lambda E: E.matmul(
                    p_ca1[:, :], c_ones[0:1, 0:NT], carry0[:, 512:1024], start=False, stop=True))
                pl.wait("vector", tS, t_cam)
                pl.op("vector", vS, lambda E: E.tensor_copy(carries[:, 0:512], p_ca0[:, :]))
                t_car = pl.op("vector", vS, lambda E: E.tensor_copy(carries[:, 512:1024], p_ca1[:, :]))
                pl.wait("scalar", vS, t_car)
                t_carf = pl.dma("scalar", qCF, carries_f[0:1, :, :].rearrange("a m d -> a (m d)"), carries[:])

            with ExitStack() as pp:
                p_pr0 = pp.enter_context(nc.psum_tensor("p_pr0", [P, 512], F32))
                p_pr1 = pp.enter_context(nc.psum_tensor("p_pr1", [P, 512], F32))
                pl.wait("tensor", qCF, t_carf)
                for m in range(NT):
                    if m >= 1:
                        pl.wait("tensor", vS, t_prefix_evac[m - 1])
                    pl.op("tensor", tS, lambda E, m=m: E.matmul(
                        p_pr0[:, :], c_trilT[:], ln0[:, m, 0:512], start=True, stop=False))
                    pl.op("tensor", tS, lambda E, m=m: E.matmul(
                        p_pr0[:, :], c_ones[0:1, 0:P], carries_f[0:1, m, 0:512], start=False, stop=True))
                    pl.op("tensor", tS, lambda E, m=m: E.matmul(
                        p_pr1[:, :], c_trilT[:], ln0[:, m, 512:1024], start=True, stop=False))
                    t_pm = pl.op("tensor", tS, lambda E, m=m: E.matmul(
                        p_pr1[:, :], c_ones[0:1, 0:P], carries_f[0:1, m, 512:1024], start=False, stop=True))
                    pl.wait("vector", tS, t_pm)
                    pl.op("vector", vS, lambda E, m=m: E.scalar_tensor_tensor(
                        xres[:, m, 0:512], p_pr0[:, :], c_invcnt[:, m:m + 1], xres[:, m, 0:512],
                        ALU.mult, ALU.add))
                    t_prefix_evac.append(pl.op("vector", vS, lambda E, m=m: E.scalar_tensor_tensor(
                        xres[:, m, 512:1024], p_pr1[:, :], c_invcnt[:, m:m + 1], xres[:, m, 512:1024],
                        ALU.mult, ALU.add)))
                t_l0a_tensor = t_pm
            if dump:
                t_tap1 = []
                for m in range(NT):
                    pl.wait("gpsimd", vS, t_prefix_evac[m])
                    t_tap1.append(pl.dma("gpsimd", qDP, tp_x1[P * m:P * (m + 1), :], xres[:, m, :]))
                pl.dma("gpsimd", qDP, tp_ln0, ln0[:].rearrange("p m d -> p (m d)"))
                pl.dma("gpsimd", qDP, tp_cc[0:NT, :], colsums[:])
                pl.dma("gpsimd", qDP, tp_cc[NT:2 * NT, :], carries[:])
                pl.dma("gpsimd", qDP, tp_cc[2 * NT:2 * NT + 1, :], carry0[:])
                t_tap1.append(pl.dma("gpsimd", qDP, tp_cc[2 * NT + 1:2 * NT + 2, :], bs_row[:]))

        t_x2 = []
        with ExitStack() as ph:
            pe = ph.enter_context
            ln1_fm = pe(nc.sbuf_tensor("ln1_fm", [P, 8, DC, P], BF16))
            ln1_tm = pe(nc.sbuf_tensor("ln1_tm", [P, 2, D], BF16))
            h0_fm = pe(nc.sbuf_tensor("h0_fm", [P, H0C, 1024], BF16))
            w0g_sb = pe(nc.sbuf_tensor("w0g_sb", [P, 2, D], BF16))
            w0u_sb = pe(nc.sbuf_tensor("w0u_sb", [P, 2, D], BF16))
            w0d_sb = pe(nc.sbuf_tensor("w0d_sb", [P, H0C, D], BF16))
            silu_sb = pe(nc.sbuf_tensor("silu_sb", [P, 2, 512], BF16))
            p_g = [pe(nc.psum_tensor("p_g%d" % b, [P, 512], F32)) for b in range(2)]
            p_u = [pe(nc.psum_tensor("p_u%d" % b, [P, 512], F32)) for b in range(2)]
            p_d = [pe(nc.psum_tensor("p_d%d" % b, [P, 512], F32)) for b in range(4)]

            t_w0d = 0
            for j in range(H0C):
                if j == 0:
                    pl.wait("gpsimd", tS, t_l0a_tensor)
                t_w0d = pl.dma("gpsimd", qWD0, w0d_sb[:, j, :], w0d_in[j])

            t_ln1tr = {}
            t_gu_first = {}
            t_gu_mm = {}
            t_hev = {}
            t_down_last = {}

            def emit_ln1_half(half):
                def _ln1_pre(k):
                    m = 8 * half + k
                    if m >= 1:
                        pl.wait("vector", qT1, t_ln1tr[m - 1])
                    if m == 0:
                        pl.wait("vector", tS, t_l0a_tensor)

                def _ln1_post(k, t_d):
                    m = 8 * half + k
                    pl.wait("sync", vS, t_d)
                    if half == 1 and k == 0:
                        pl.wait("sync", tS, t_gu_mm[(0, H0C - 1)])
                    t_ln1tr[m] = pl.dmat("sync", qT1, ln1_fm[:, k, :, :], ln1_tm[:, m % 2, :])

                emit_ln_batch(
                    [(xres[:, 8 * half + k, :], ln1_tm[:, (8 * half + k) % 2, :],
                      [(vS, t_prefix_evac[8 * half + k])]) for k in range(8)],
                    dst_pre=_ln1_pre, post_dst=_ln1_post)

            def emit_gu_half(half):
                nonlocal_state = {}
                for i in range(H0C):
                    ib = i % 2
                    prevc = (half, i - 1) if i >= 1 else ((0, H0C - 1) if half else None)
                    if prevc is not None:
                        pl.wait("gpsimd", tS, t_gu_first[prevc])
                    tg_ = pl.dma("gpsimd", qWG0, w0g_sb[:, ib, :], w0g_in[i])
                    tu_ = pl.dma("gpsimd", qWU0, w0u_sb[:, ib, :], w0u_in[i])
                    pl.wait("tensor", qWG0, tg_)
                    pl.wait("tensor", qWU0, tu_)
                    if i == 0:
                        pl.wait("tensor", qT1, t_ln1tr[8 * half + 7])
                        if half == 0:
                            pl.wait("tensor", vS, t_prefix_evac[-1])
                    first_of_chunk = True
                    for tg in range(2):
                        slot = 2 * i + tg
                        pb = slot % 2
                        prevs = (half, (slot - 2) // 2, (slot - 2) % 2) if slot >= 2 else \
                                ((half - 1, (2 * H0C + slot - 2) // 2, (2 * H0C + slot - 2) % 2) if half else None)
                        if prevs is not None:
                            pl.wait("tensor", vS, t_hev[prevs])
                        for c in range(DC):
                            t_mmg = pl.op("tensor", tS, lambda E, tg=tg, c=c, ib=ib, pb=pb: E.matmul(
                                p_g[pb][:, :], w0g_sb[:, ib, P * c:P * (c + 1)],
                                ln1_fm[:, 4 * tg:4 * tg + 4, c, :],
                                start=(c == 0), stop=(c == DC - 1)))
                            if first_of_chunk:
                                t_gu_first[(half, i)] = t_mmg
                                first_of_chunk = False
                        for c in range(DC):
                            t_mmu = pl.op("tensor", tS, lambda E, tg=tg, c=c, ib=ib, pb=pb: E.matmul(
                                p_u[pb][:, :], w0u_sb[:, ib, P * c:P * (c + 1)],
                                ln1_fm[:, 4 * tg:4 * tg + 4, c, :],
                                start=(c == 0), stop=(c == DC - 1)))
                        pl.wait("scalar", tS, t_mmu)
                        t_si = pl.op("scalar", sS, lambda E, pb=pb: E.activation(
                            silu_sb[:, pb, :], p_g[pb][:, :], AF.Silu, scale=INV127))
                        pl.wait("vector", sS, t_si)
                        if half == 1 and slot == 0:
                            pl.wait("vector", tS, t_down_last[0])
                        t_hev[(half, i, tg)] = pl.op("vector", vS, lambda E, i=i, tg=tg, pb=pb: E.tensor_tensor(
                            h0_fm[:, i, 512 * tg:512 * (tg + 1)], silu_sb[:, pb, :], p_u[pb][:, :], ALU.mult))
                    t_gu_mm[(half, i)] = t_mmu

            def emit_down_half(half):
                t_dm = 0
                for mt in range(8):
                    m = 8 * half + mt
                    pb0, pb1 = 2 * (mt % 2), 2 * (mt % 2) + 1
                    if m >= 2:
                        pl.wait("tensor", vS, t_x2[m - 2])
                    for j in range(H0C):
                        if mt == 0 and j == 0:
                            pl.wait("tensor", vS, t_hev[(half, H0C - 1, 1)])
                            pl.wait("tensor", qWD0, t_w0d)
                        pl.op("tensor", tS, lambda E, j=j, mt=mt, pb0=pb0: E.matmul(
                            p_d[pb0][:, :], h0_fm[:, j, P * mt:P * (mt + 1)], w0d_sb[:, j, 0:512],
                            start=(j == 0), stop=(j == H0C - 1)))
                        t_dm = pl.op("tensor", tS, lambda E, j=j, mt=mt, pb1=pb1: E.matmul(
                            p_d[pb1][:, :], h0_fm[:, j, P * mt:P * (mt + 1)], w0d_sb[:, j, 512:1024],
                            start=(j == 0), stop=(j == H0C - 1)))
                    pl.wait("vector", tS, t_dm)
                    if dump:
                        pl.wait("vector", qDP, t_tap1[-1])
                    pl.op("vector", vS, lambda E, m=m, pb0=pb0: E.scalar_tensor_tensor(
                        xres[:, m, 0:512], p_d[pb0][:, :], INV127SQ, xres[:, m, 0:512], ALU.mult, ALU.add))
                    t_x2.append(pl.op("vector", vS, lambda E, m=m, pb1=pb1: E.scalar_tensor_tensor(
                        xres[:, m, 512:1024], p_d[pb1][:, :], INV127SQ, xres[:, m, 512:1024],
                        ALU.mult, ALU.add)))
                    if dump:
                        pl.wait("gpsimd", vS, t_x2[-1])
                        pl.dma("gpsimd", qDP, tp_x2[P * m:P * (m + 1), :], xres[:, m, :])
                t_down_last[half] = t_dm
                return t_dm

            emit_ln1_half(0)
            emit_gu_half(0)
            emit_ln1_half(1)
            emit_down_half(0)
            emit_gu_half(1)
            t_dm = emit_down_half(1)
            t_l0f_tensor = t_dm

        ph3arena = ExitStack()
        pe34 = ph3arena.enter_context
        yq_ext = pe34(nc.sbuf_tensor("yq_ext", [R + 2, TOK], BF16))
        yT_all = pe34(nc.sbuf_tensor("yT_all", [R + 2, NCORE, TOK], BF16))

        t_y_frontier = None
        with ExitStack() as ph:
            pe = ph.enter_context
            v_tm = pe(nc.sbuf_tensor("v_tm", [P, 2, D], BF16))
            v_fm = pe(nc.sbuf_tensor("v_fm", [P, DC, TOK], BF16))
            yT_f = pe(nc.sbuf_tensor("yT_f", [R, TOK], F32))
            y2_b = pe(nc.sbuf_tensor("y2_b", [R, TOK], BF16))
            nrow = pe(nc.sbuf_tensor("nrow", [1, TOK], BF16))

            t_vtr = {}
            vstate = {"stage": 0}

            def _v_pre(k):
                if k >= 1:
                    pl.wait("vector", qVS, 16 * k)
                    pl.wait("vector", qTV, t_vtr[k - 1])
                if k == 0:
                    pl.wait("vector", tS, t_l0f_tensor)
                    pl.wait("sync", tS, t_l0f_tensor)
                    pl.wait("gpsimd", tS, t_l0f_tensor)

            def _v_post(k, t_v):
                pl.wait("gpsimd", vS, t_v)
                vstate["stage"] = pl.dma("gpsimd", qVS, d_v_in.ap()[P * k:P * (k + 1), :],
                                         v_tm[:, k % 2, :])
                pl.wait("sync", vS, t_v)
                t_vtr[k] = pl.dmat("sync", qTV, v_fm[:, :, P * k:P * (k + 1)], v_tm[:, k % 2, :])

            emit_ln_batch([(xres[:, m, :], v_tm[:, m % 2, :], [(vS, t_x2[m])])
                           for m in range(NT)], dst_pre=_v_pre, post_dst=_v_post)
            pl.wait("gpsimd", qVS, vstate["stage"])
            t_ag_v = pl.op("gpsimd", cV, lambda E: E.collective_compute(
                "AllGather", ALU.bypass, replica_groups=[list(range(NCORE))],
                ins=[d_v_in.ap().opt()], outs=[d_vgat.ap().opt()]))

            pl.op("vector", vS, lambda E: E.tensor_copy(
                uUb[:].rearrange("p c r -> p (c r)"), uU[:].rearrange("p c r -> p (c r)")))
            with ExitStack() as pp:
                p_y = pp.enter_context(nc.psum_tensor("p_y", [R, 512], F32))
                p_n = pp.enter_context(nc.psum_tensor("p_n", [1, 512], F32))
                pl.wait("tensor", qTV, t_vtr[NT - 1])
                pl.wait("tensor", qCN, t_consts)
                t_ye = 0
                for tg in range(4):
                    if tg >= 1:
                        pl.wait("tensor", vS, t_ye)
                    for c in range(DC):
                        t_ym = pl.op("tensor", tS, lambda E, tg=tg, c=c: E.matmul(
                            p_y[:, :], uUb[:, c, :], v_fm[:, c, 512 * tg:512 * (tg + 1)],
                            start=(c == 0), stop=(c == DC - 1)))
                    pl.wait("vector", tS, t_ym)
                    t_ye = pl.op("vector", vS, lambda E, tg=tg: E.tensor_copy(
                        yT_f[:, 512 * tg:512 * (tg + 1)], p_y[:, :]))
                pl.op("vector", vS, lambda E: E.tensor_scalar(
                    yq_ext[0:R, :], yT_f[:, :], 2.0, None, ALU.mult))
                t_y2 = pl.op("vector", vS, lambda E: E.tensor_tensor(
                    y2_b[:, :], yT_f[:, :], yT_f[:, :], ALU.mult))
                pl.wait("tensor", vS, t_y2)
                t_ne = 0
                for tg in range(4):
                    if tg >= 1:
                        pl.wait("tensor", vS, t_ne)
                    t_nm = pl.op("tensor", tS, lambda E, tg=tg: E.matmul(
                        p_n[:, :], c_ones[0:R, 0:1], y2_b[:, 512 * tg:512 * (tg + 1)],
                        start=True, stop=True))
                    pl.wait("vector", tS, t_nm)
                    t_ne = pl.op("vector", vS, lambda E, tg=tg: E.tensor_scalar(
                        nrow[:, 512 * tg:512 * (tg + 1)], p_n[:, :], -1.0, None, ALU.mult))
                t_y_frontier = t_nm
            t_ms = pl.op("vector", vS, lambda E: E.memset(yq_ext[R:R + 1, :], 1.0))
            pl.wait("scalar", vS, t_ne)
            t_nq = pl.dma("scalar", qNQ, yq_ext[R + 1:R + 2, :], nrow[:])
            t_yb = pl.op("vector", vS, lambda E: E.tensor_copy(y2_b[:, :], yT_f[:, :]))
            pl.wait("gpsimd", vS, t_yb)
            pl.dma("gpsimd", qYS, d_y_in.ap()[0:R, :], y2_b[:, :])
            pl.wait("gpsimd", vS, t_ne)
            t_n32 = pl.dma("gpsimd", qYS, d_y_in.ap()[R:R + 1, :], nrow[:])
            pl.wait("vector", qYS, t_n32)
            pl.wait("vector", qNQ, t_nq)
            t_on = pl.op("vector", vS, lambda E: E.memset(nrow[:, :], 1.0))
            pl.wait("gpsimd", vS, t_on)
            t_ystage = pl.dma("gpsimd", qYS, d_y_in.ap()[R + 1:R + 2, :], nrow[:])
            pl.wait("gpsimd", qYS, t_ystage)
            t_ag_y = pl.op("gpsimd", cY, lambda E: E.collective_compute(
                "AllGather", ALU.bypass, replica_groups=[list(range(NCORE))],
                ins=[d_y_in.ap().opt()], outs=[d_y_out.ap().opt()]))
            pl.wait("gpsimd", cY, t_ag_y)
            t_yall = 0
            for cb in range(NCORE):
                t_yall = pl.dma("gpsimd", qYL, yT_all[:, cb, :],
                                d_y_out.ap()[(R + 2) * cb:(R + 2) * (cb + 1), :])

        wprep = ExitStack()
        pw = wprep.enter_context
        wf_sb = pw(nc.sbuf_tensor("wf_sb", [P, 2, H1], F32))
        wq_nat = pw(nc.sbuf_tensor("wq_nat", [P, 2, H1], BF16))
        wq_i32 = pw(nc.sbuf_tensor("wq_i32", [P, H1], I32))
        wq_st = pw(nc.sbuf_tensor("wq_st", [P, H1C, P], BF16, is_transpose_dest=True))
        wst = pw(nc.sbuf_tensor("wst", [P, 16], F32))

        JOBS = [("g", i) for i in range(H1C)] + [("u", i) for i in range(H1C)] + \
               [("d", c) for c in range(DC)]
        job_state = {"done": 0}
        t_job_r = {}
        t_job_v = {}
        t_job_tr = {}
        t_job_sp = {}

        def emit_wjob(j):
            kind, i = JOBS[j]
            buf = j % 2
            src = {"g": w1g_in, "u": w1u_in, "d": w1d_in}[kind]
            ncols = D if kind in ("g", "u") else H1
            nch = DC if kind in ("g", "u") else H1C
            if j == 0:
                pl.wait("gpsimd", tS, t_y_frontier)
            else:
                pl.wait("gpsimd", vS, t_job_r[j - 1])
            t_ld = pl.dma("gpsimd", qWJ, wf_sb[:, buf, 0:ncols], src[P * i:P * (i + 1), :])
            pl.wait("vector", qWJ, t_ld)
            t_job_r[j] = pl.op("vector", vS, lambda E, buf=buf, n=ncols: E.reduce_max(
                wst[:, 0:1], wf_sb[:, buf, 0:n], axis=AX.X, apply_absolute_value=True))
            pl.wait("gpsimd", vS, t_job_r[j])
            t_gsc = pl.op("gpsimd", gS, lambda E: E.tensor_scalar(
                wst[:, 1:2], wst[:, 0:1], INV127, None, ALU.mult))
            pl.wait("vector", gS, t_gsc)
            pl.op("vector", vS, lambda E: E.reciprocal(wst[:, 2:3], wst[:, 1:2]))
            if j >= 1:
                pl.wait("vector", qWT, t_job_tr[j - 1])
            pl.op("vector", vS, lambda E, buf=buf, n=ncols: E.tensor_scalar(
                wq_i32[:, 0:n], wf_sb[:, buf, 0:n], wst[:, 2:3], None, ALU.mult))
            t_q = pl.op("vector", vS, lambda E, buf=buf, n=ncols: E.tensor_scalar(
                wq_nat[:, buf, 0:n], wq_i32[:, 0:n], wst[:, 1:2], None, ALU.mult))
            t_job_v[j] = t_q
            pl.wait("sync", vS, t_q)
            if j == 0:
                pl.wait("sync", qNQ, t_nq)
            if j >= 1:
                pl.wait("sync", gS, j)
                pl.wait("sync", qSP, t_job_sp[j - 1])
            t_tr = pl.dmat("sync", qWT, wq_st[:, 0:nch, :], wq_nat[:, buf, 0:ncols])
            t_job_tr[j] = t_tr
            pl.wait("gpsimd", qWT, t_tr)
            if kind == "g":
                t_sp = pl.dma("gpsimd", qSP, d_wgq[i], wq_st[:, 0:nch, :].rearrange("p c k -> p (c k)"))
            elif kind == "u":
                t_sp = pl.dma("gpsimd", qSP, d_wuq[i], wq_st[:, 0:nch, :].rearrange("p c k -> p (c k)"))
            else:
                t_sp = pl.dma("gpsimd", qSP, d_wdq[:, :, P * i:P * (i + 1)].rearrange("j p k -> p j k"),
                              wq_st[:, 0:nch, :])
            t_job_sp[j] = t_sp
            pl.eng_inc("gpsimd", gS)

        def emit_wjobs(n):
            while n > 0 and job_state["done"] < len(JOBS):
                emit_wjob(job_state["done"])
                job_state["done"] += 1
                n -= 1

        t_x3w = []
        with ExitStack() as ph:
            pe = ph.enter_context
            p_out = [[pe(nc.psum_tensor("p_o%d_%d" % (b, h2), [P, 512], F32)) for h2 in range(2)]
                     for b in range(3)]
            p_sc = pe(nc.psum_tensor("p_sc", [P, 384], F32))
            p_z = pe(nc.psum_tensor("p_z", [P, 16], F32))
            P_sb = pe(nc.sbuf_tensor("P_sb", [P, 2, 384], BF16))
            v_str = pe(nc.sbuf_tensor("v_str", [P, 2, 8, D], BF16))
            rz_sb = pe(nc.sbuf_tensor("rz_sb", [P, 16], F32))
            x3_st = pe(nc.sbuf_tensor("x3_st", [P, 3, D], F32))

            emit_wjobs(8)

            KT = NCORE * TOK // P
            G8 = KT // 8
            pl.wait("sync", cV, t_ag_v)
            pl.wait("tensor", qYL, t_yall)
            pl.wait("tensor", qNQ, t_nq)
            pl.wait("tensor", vS, t_ms)

            t_pv_prev = None
            t_ln3_stats = {}
            for g, (qoff, nqb) in enumerate(QG_GROUPS):
                nq = 128 * nqb
                t_score = {}
                t_pv = {}
                t_vload = {}
                sS_base = sS.v
                if g > 0:
                    pl.wait("tensor", vS, t_x3w[-1][0])

                def mk_score(kt):
                    cblk, off = kt // NT, (kt % NT) * P
                    return pl.op("tensor", tS, lambda E, cblk=cblk, off=off, qoff=qoff, nq=nq: E.matmul(
                        p_sc[:, 0:nq], yT_all[:, cblk, off:off + P], yq_ext[:, qoff:qoff + nq],
                        start=True, stop=True))

                for kt in range(KT):
                    g8, j8, vbuf = kt // 8, kt % 8, (kt // 8) % 2
                    pb = kt % 2
                    if j8 == 0:
                        if g8 >= 1:
                            pl.wait("sync", tS, t_pv[8 * (g8 - 1)])
                        elif g > 0:
                            pl.wait("sync", tS, t_pv_prev[8 * (G8 - 1)])
                        rows = d_vgat.ap()[1024 * g8:1024 * (g8 + 1), :]
                        t_vload[g8] = pl.dma("sync", qVL, v_str[:, vbuf, :, :],
                                             rows.rearrange("(j p) f -> p j f", p=P))
                        if g + g8 > 0:
                            emit_wjobs(1)
                    if kt == 0:
                        t_score[0] = mk_score(0)
                    pl.wait("scalar", tS, t_score[kt])
                    pl.op("scalar", sS, lambda E, pb=pb, nq=nq: E.activation(
                        P_sb[:, pb, 0:nq], p_sc[:, 0:nq], AF.Exp))
                    pl.wait("tensor", sS, sS_base + kt + 1)
                    if kt + 1 < KT:
                        t_score[kt + 1] = mk_score(kt + 1)
                    if j8 == 0:
                        pl.wait("tensor", qVL, t_vload[g8])
                    first, last = (kt == 0), (kt == KT - 1)
                    for b in range(nqb):
                        pl.op("tensor", tS, lambda E, b=b, pb=pb, j8=j8, vbuf=vbuf, first=first, last=last: E.matmul(
                            p_out[b][0][:, :], P_sb[:, pb, P * b:P * (b + 1)],
                            v_str[:, vbuf, j8, 0:512], start=first, stop=last))
                        pl.op("tensor", tS, lambda E, b=b, pb=pb, j8=j8, vbuf=vbuf, first=first, last=last: E.matmul(
                            p_out[b][1][:, :], P_sb[:, pb, P * b:P * (b + 1)],
                            v_str[:, vbuf, j8, 512:1024], start=first, stop=last))
                        t_pv[kt] = pl.op("tensor", tS, lambda E, b=b, pb=pb, first=first, last=last, nqb=nqb: E.matmul(
                            p_z[:, b:b + 1], P_sb[:, pb, P * b:P * (b + 1)], c_ones[:, 0:1],
                            start=(first and b == 0), stop=(last and b == nqb - 1)))
                pl.wait("vector", tS, t_pv[KT - 1])
                for b in range(nqb):
                    m = (qoff // P) + b
                    sb2 = b
                    pl.op("vector", vS, lambda E, b=b: E.reciprocal(rz_sb[:, b:b + 1], p_z[:, b:b + 1]))
                    if t_x3w:
                        pl.wait("vector", qX3W, t_x3w[-1][1])
                    pl.op("vector", vS, lambda E, b=b, m=m, sb2=sb2: E.scalar_tensor_tensor(
                        x3_st[:, sb2, 0:512], p_out[b][0][:, :], rz_sb[:, b:b + 1], xres[:, m, 0:512],
                        ALU.mult, ALU.add))
                    t_ev = pl.op("vector", vS, lambda E, b=b, m=m, sb2=sb2: E.scalar_tensor_tensor(
                        x3_st[:, sb2, 512:1024], p_out[b][1][:, :], rz_sb[:, b:b + 1], xres[:, m, 512:1024],
                        ALU.mult, ALU.add))
                    pl.wait("gpsimd", vS, t_ev)
                    t_w = pl.dma("gpsimd", qX3W, d_x3[P * m:P * (m + 1), :], x3_st[:, sb2, :])
                    t_x3w.append((t_ev, t_w))
                for b in range(nqb):
                    m = (qoff // P) + b
                    sb2 = b
                    pl.op("vector", vS, lambda E, m=m, sb2=sb2: E.reduce_sum(
                        st_sum[:, m:m + 1], x3_st[:, sb2, :], axis=AX.X))
                    t_q2 = pl.op("vector", vS, lambda E, sb2=sb2: E.tensor_tensor(
                        sq_f32[:, 0, :], x3_st[:, sb2, :], x3_st[:, sb2, :], ALU.mult))
                    pl.wait("vector", vS, t_q2)
                    t_ln3_stats[m] = pl.op("vector", vS, lambda E, m=m: E.reduce_sum(
                        st_sq[:, m:m + 1], sq_f32[:, 0, :], axis=AX.X))
                t_pv_prev = t_pv
            emit_wjobs(len(JOBS))
            t_attn_tensor = t_pv_prev[KT - 1]

        wprep.close()
        ph3arena.close()
        s123.close()

        with ExitStack() as ph:
            pe = ph.enter_context
            x3_t = pe(nc.sbuf_tensor("x3_t", [P, 2, D], F32))
            mxacc = pe(nc.sbuf_tensor("mxacc", [P, 32], F32))
            mxrow = pe(nc.sbuf_tensor("mxrow", [1, P], F32))
            xq_i32 = pe(nc.sbuf_tensor("xq_i32", [P, 2, D], I32))
            xq_f32 = pe(nc.sbuf_tensor("xq_f32", [P, D], F32))
            h1_fm = pe(nc.sbuf_tensor("h1_fm", [P, H1C, TOK], BF16))
            wg_ch = pe(nc.sbuf_tensor("wg_ch", [P, 2, D], BF16))
            wu_ch = pe(nc.sbuf_tensor("wu_ch", [P, 2, D], BF16))
            silu1 = pe(nc.sbuf_tensor("silu1", [P, 2, 512], BF16))
            hmx = pe(nc.sbuf_tensor("hmx", [P, 96], F32))
            p_g1 = [pe(nc.psum_tensor("p_g1%d" % b, [P, 512], F32)) for b in range(2)]
            p_u1 = [pe(nc.psum_tensor("p_u1%d" % b, [P, 512], F32)) for b in range(2)]
            ln3_t = pe(nc.sbuf_tensor("ln3_t", [P, 2, D], BF16))
            ph4b = ExitStack()
            xq_tm = ph4b.enter_context(nc.sbuf_tensor("xq_tm", [P, 2, D], BF16))
            xq_fm = ph4b.enter_context(nc.sbuf_tensor("xq_fm", [P, NT, DC, P], BF16))

            pl.wait("vector", qSP, t_job_sp[len(JOBS) - 1])
            pl.wait("sync", qX3W, t_x3w[-1][1])
            pl.wait("sync", vS, t_job_v[len(JOBS) - 1])
            pl.wait("sync", qWT, t_job_tr[len(JOBS) - 1])
            pl.wait("sync", qWJ, qWJ.v)
            pl.wait("gpsimd", vS, t_ln3_stats[NT - 1])
            pl.op("gpsimd", gS, lambda E: E.tensor_scalar(
                st_w[:, 0, 0:NT], st_sum[:, 0:NT], 1.0 / D, None, ALU.mult))
            pl.op("gpsimd", gS, lambda E: E.tensor_scalar(
                st_w[:, 1, 0:NT], st_sq[:, 0:NT], 1.0 / D, None, ALU.mult))
            pl.op("gpsimd", gS, lambda E: E.tensor_tensor(
                st_w[:, 2, 0:NT], st_w[:, 0, 0:NT], st_w[:, 0, 0:NT], ALU.mult))
            pl.op("gpsimd", gS, lambda E: E.tensor_tensor(
                st_w[:, 3, 0:NT], st_w[:, 1, 0:NT], st_w[:, 2, 0:NT], ALU.subtract))
            t_gv3 = pl.op("gpsimd", gS, lambda E: E.tensor_scalar(
                st_w[:, 4, 0:NT], st_w[:, 3, 0:NT], EPS, None, ALU.add))
            pl.wait("scalar", gS, t_gv3)
            t_sd3 = pl.op("scalar", sS, lambda E: E.activation(
                st_w[:, 5, 0:NT], st_w[:, 4, 0:NT], AF.Sqrt))
            pl.wait("vector", sS, t_sd3)
            pl.op("vector", vS, lambda E: E.reciprocal(st_w[:, 6, 0:NT], st_w[:, 5, 0:NT]))
            t_d3 = 0
            t_stat3 = 0
            for m in range(NT):
                buf = m % 2
                pl.wait("sync", qX3W, t_x3w[m][1])
                if m >= 2:
                    pl.wait("sync", vS, t_stat3)
                t_ld3 = pl.dma("sync", qX3L, x3_t[:, buf, :], d_x3[P * m:P * (m + 1), :])
                pl.wait("vector", qX3L, t_ld3)
                t_d3 = pl.op("vector", vS, lambda E, m=m, buf=buf: E.tensor_scalar(
                    ln3_t[:, buf, :], x3_t[:, buf, :], st_w[:, 0, m:m + 1], st_w[:, 6, m:m + 1],
                    ALU.subtract, ALU.mult))
                pl.wait("vector", vS, t_d3)
                t_stat3 = pl.op("vector", vS, lambda E, m=m, buf=buf: E.reduce_max(
                    mxacc[:, m:m + 1], ln3_t[:, buf, :], axis=AX.X, apply_absolute_value=True))
            pl.wait("vector", vS, t_stat3)
            t_rf = pl.op("vector", vS, lambda E: E.reduce_max(
                mxacc[:, 16:17], mxacc[:, 0:NT], axis=AX.X))
            pl.wait("scalar", vS, t_rf)
            pl.wait("scalar", qWT, t_job_tr[len(JOBS) - 1])
            t_mrow = pl.dma("scalar", qMW, mxrow[:, :], mxacc[:, 16:17])
            pl.wait("vector", qMW, t_mrow)
            t_rg = pl.op("vector", vS, lambda E: E.reduce_max(mx_sb[0:1, 0:1], mxrow[0:1, :], axis=AX.X))
            pl.wait("vector", vS, t_rg)
            t_g1 = pl.op("vector", vS, lambda E: E.memset(mx_sb[0:1, 1:8], 0.0))
            pl.wait("gpsimd", vS, t_g1)
            t_mst = pl.dma("gpsimd", qM1a, d_mx1_in.ap(), mx_sb[0:1, 0:8])
            pl.wait("gpsimd", qM1a, t_mst)
            t_ar1 = pl.op("gpsimd", cM1, lambda E: E.collective_compute(
                "AllReduce", ALU.max, replica_groups=[list(range(NCORE))],
                ins=[d_mx1_in.ap().opt()], outs=[d_mx1_out.ap().opt()]))
            pl.wait("gpsimd", cM1, t_ar1)
            t_mld = pl.dma("gpsimd", qM1b, mx_sb[0:1, 0:8], d_mx1_out.ap())
            pl.wait("scalar", qM1b, t_mld)
            pl.wait("scalar", qCN, t_consts)
            t_sg = pl.op("scalar", sS, lambda E: E.activation(scal_sb[0:1, 0:1], sclip[0:1, 0:1], AF.Sigmoid))
            pl.wait("gpsimd", sS, t_sg)
            pl.op("gpsimd", gS, lambda E: E.tensor_tensor(
                scal_sb[0:1, 1:2], scal_sb[0:1, 0:1], mx_sb[0:1, 0:1], ALU.mult))
            t_gsx = pl.op("gpsimd", gS, lambda E: E.tensor_scalar(
                scal_sb[0:1, 2:3], scal_sb[0:1, 1:2], INV127, None, ALU.mult))
            pl.wait("vector", gS, t_gsx)
            t_sx = pl.op("vector", vS, lambda E: E.reciprocal(scal_sb[0:1, 3:4], scal_sb[0:1, 2:3]))
            pl.wait("tensor", vS, t_sx)
            pl.wait("tensor", qCN, t_consts)
            t_bc = pl.op("tensor", tS, lambda E: E.matmul(
                p_g1[0][:, 0:2], c_onesf[:], scal_sb[0:1, 2:4], start=True, stop=True))
            pl.wait("vector", tS, t_bc)
            t_sxb = pl.op("vector", vS, lambda E: E.tensor_copy(sxb[:, 0:2], p_g1[0][:, 0:2]))

            t_xq = {}
            t_xc = 0
            for m in range(NT):
                buf = m % 2
                if m == 0:
                    pl.wait("vector", vS, t_sxb)
                if m >= 2:
                    pl.wait("sync", vS, t_xc)
                t_lq = pl.dma("sync", qX3D, x3_t[:, buf, :], d_x3[P * m:P * (m + 1), :])
                if m >= 1:
                    pl.wait("vector", qTX, t_xq[m - 1])
                pl.wait("vector", qX3D, t_lq)
                pl.op("vector", vS, lambda E, m=m, buf=buf: E.tensor_scalar(
                    ln3_t[:, buf, :], x3_t[:, buf, :], st_w[:, 0, m:m + 1], st_w[:, 6, m:m + 1],
                    ALU.subtract, ALU.mult))
                pl.op("vector", vS, lambda E, buf=buf: E.tensor_scalar(
                    xq_f32[:, :], ln3_t[:, buf, :], sxb[:, 1:2], 127.0, ALU.mult, ALU.min))
                pl.op("vector", vS, lambda E: E.tensor_single_scalar(
                    xq_i32[:, 0, :], xq_f32[:, :], -128.0, ALU.max))
                t_c = pl.op("vector", vS, lambda E, buf=buf: E.tensor_copy(
                    xq_tm[:, buf, :], xq_i32[:, 0, :]))
                t_xc = t_c
                pl.wait("sync", vS, t_c)
                t_xq[m] = pl.dmat("sync", qTX, xq_fm[:, m, :, :], xq_tm[:, buf, :])

            t_h1 = {}
            t_humx = {}
            t_gu1_first = {}
            first_silu = True
            for i in range(H1C):
                ib = i % 2
                if i == 0:
                    pl.wait("sync", tS, t_attn_tensor)
                    pl.wait("sync", qSP, t_job_sp[len(JOBS) - 1])
                else:
                    pl.wait("sync", tS, t_gu1_first[i - 1])
                t_g = pl.dma("sync", qWQL, wg_ch[:, ib, :], d_wgq[i])
                t_u = pl.dma("sync", qWQL, wu_ch[:, ib, :], d_wuq[i])
                pl.wait("tensor", qWQL, t_u)
                if i == 0:
                    pl.wait("tensor", qTX, t_xq[NT - 1])
                    pl.wait("tensor", vS, t_sxb)
                for tg in range(4):
                    slot = 4 * i + tg
                    pb = slot % 2
                    if slot >= 2:
                        pl.wait("tensor", vS, t_h1[((slot - 2) // 4, (slot - 2) % 4, "h")])
                    first_of_chunk = (tg == 0)
                    for c in range(DC):
                        t_mmg = pl.op("tensor", tS, lambda E, tg=tg, c=c, ib=ib, pb=pb: E.matmul(
                            p_g1[pb][:, :], wg_ch[:, ib, P * c:P * (c + 1)],
                            xq_fm[:, 4 * tg:4 * tg + 4, c, :], start=(c == 0), stop=(c == DC - 1)))
                        if first_of_chunk:
                            t_gu1_first[i] = t_mmg
                            first_of_chunk = False
                    for c in range(DC):
                        t_mmu = pl.op("tensor", tS, lambda E, tg=tg, c=c, ib=ib, pb=pb: E.matmul(
                            p_u1[pb][:, :], wu_ch[:, ib, P * c:P * (c + 1)],
                            xq_fm[:, 4 * tg:4 * tg + 4, c, :], start=(c == 0), stop=(c == DC - 1)))
                    t_h1[(i, tg, "mm")] = t_mmu
                    pl.wait("scalar", tS, t_mmu)
                    if first_silu:
                        pl.wait("scalar", vS, t_sxb)
                        first_silu = False
                    t_si = pl.op("scalar", sS, lambda E, pb=pb: E.activation(
                        silu1[:, pb, :], p_g1[pb][:, :], AF.Silu, scale=sxb[:, 0:1]))
                    pl.wait("vector", sS, t_si)
                    t_h = pl.op("vector", vS, lambda E, i=i, tg=tg, pb=pb: E.tensor_tensor(
                        h1_fm[:, i, 512 * tg:512 * (tg + 1)], silu1[:, pb, :], p_u1[pb][:, :], ALU.mult))
                    t_h1[(i, tg, "h")] = t_h
                    pl.wait("vector", vS, t_h)
                    t_humx[(i, tg)] = pl.op("vector", vS, lambda E, i=i, tg=tg: E.reduce_max(
                        hmx[:, 4 * i + tg:4 * i + tg + 1], h1_fm[:, i, 512 * tg:512 * (tg + 1)],
                        axis=AX.X, apply_absolute_value=True))

            pl.wait("vector", vS, t_humx[(H1C - 1, 3)])
            t_rhf = pl.op("vector", vS, lambda E: E.reduce_max(
                hmx[:, 88:89], hmx[:, 0:4 * H1C], axis=AX.X))
            pl.wait("scalar", vS, t_rhf)
            t_hrow = pl.dma("scalar", qHW, mxrow[:, :], hmx[:, 88:89])
            pl.wait("vector", qHW, t_hrow)
            t_rg2 = pl.op("vector", vS, lambda E: E.reduce_max(mxh_sb[0:1, 0:1], mxrow[0:1, :], axis=AX.X))
            pl.wait("vector", vS, t_rg2)
            t_g2 = pl.op("vector", vS, lambda E: E.memset(mxh_sb[0:1, 1:8], 0.0))
            pl.wait("gpsimd", vS, t_g2)
            t_hst = pl.dma("gpsimd", qM2a, d_mx2_in.ap(), mxh_sb[0:1, 0:8])
            pl.wait("gpsimd", qM2a, t_hst)
            t_ar2 = pl.op("gpsimd", cM2, lambda E: E.collective_compute(
                "AllReduce", ALU.max, replica_groups=[list(range(NCORE))],
                ins=[d_mx2_in.ap().opt()], outs=[d_mx2_out.ap().opt()]))
            pl.wait("gpsimd", cM2, t_ar2)
            t_hld = pl.dma("gpsimd", qM2b, mxh_sb[0:1, 0:8], d_mx2_out.ap())
            pl.wait("scalar", qM2b, t_hld)
            t_sg2 = pl.op("scalar", sS, lambda E: E.activation(scal_sb[0:1, 4:5], sclip[0:1, 1:2], AF.Sigmoid))
            pl.wait("gpsimd", sS, t_sg2)
            pl.op("gpsimd", gS, lambda E: E.tensor_tensor(
                scal_sb[0:1, 5:6], mxh_sb[0:1, 0:1], scal_sb[0:1, 2:3], ALU.mult))
            pl.op("gpsimd", gS, lambda E: E.tensor_tensor(
                scal_sb[0:1, 6:7], scal_sb[0:1, 5:6], scal_sb[0:1, 4:5], ALU.mult))
            t_gsh = pl.op("gpsimd", gS, lambda E: E.tensor_scalar(
                scal_sb[0:1, 7:8], scal_sb[0:1, 6:7], INV127, None, ALU.mult))
            pl.wait("vector", gS, t_gsh)
            t_rsh = pl.op("vector", vS, lambda E: E.reciprocal(scal_sb[0:1, 9:10], scal_sb[0:1, 7:8]))
            pl.wait("gpsimd", vS, t_rsh)
            t_fq = pl.op("gpsimd", gS, lambda E: E.tensor_tensor(
                scal_sb[0:1, 8:9], scal_sb[0:1, 2:3], scal_sb[0:1, 9:10], ALU.mult))
            pl.wait("tensor", gS, t_fq)
            t_bc2 = pl.op("tensor", tS, lambda E: E.matmul(
                p_g1[0][:, 2:4], c_onesf[:], scal_sb[0:1, 7:9], start=True, stop=True))
            pl.wait("vector", tS, t_bc2)
            t_shb = pl.op("vector", vS, lambda E: E.tensor_copy(shb[:, 0:2], p_g1[0][:, 2:4]))

            t_hq = 0
            hq_sS = []
            for i in range(H1C):
                if i == 0:
                    pl.wait("vector", vS, t_shb)
                for hh in range(2):
                    ib2 = (2 * i + hh) % 2
                    if 2 * i + hh >= 2:
                        pl.wait("vector", sS, hq_sS[2 * i + hh - 2])
                    pl.op("vector", vS, lambda E, i=i, hh=hh: E.tensor_scalar(
                        xq_f32[:, :], h1_fm[:, i, 1024 * hh:1024 * (hh + 1)], shb[:, 1:2], 127.0,
                        ALU.mult, ALU.min))
                    t_cv = pl.op("vector", vS, lambda E, ib2=ib2: E.tensor_single_scalar(
                        xq_i32[:, ib2, :], xq_f32[:, :], -128.0, ALU.max))
                    pl.wait("scalar", vS, t_cv)
                    t_hq = pl.op("scalar", sS, lambda E, i=i, hh=hh, ib2=ib2: E.copy(
                        h1_fm[:, i, 1024 * hh:1024 * (hh + 1)], xq_i32[:, ib2, :]))
                    hq_sS.append(t_hq)
            ph4b.close()
            wd_sb = pe(nc.sbuf_tensor("wd_sb", [P, H1C, D], BF16))
            y_st = pe(nc.sbuf_tensor("y_st", [P, 2, D], F32))
            p_d1 = [pe(nc.psum_tensor("p_d1%d" % b, [P, 512], F32)) for b in range(4)]
            t_wd = 0
            for j in range(H1C):
                if j == 0:
                    pl.wait("sync", qSP, t_job_sp[len(JOBS) - 1])
                    pl.wait("sync", tS, t_h1[(H1C - 1, 3, "mm")])
                t_wd = pl.dma("sync", qWDL, wd_sb[:, j, :], d_wdq[j])
            t_yw = []
            for m in range(NT):
                buf = m % 2
                pb0, pb1 = 2 * (m % 2), 2 * (m % 2) + 1
                if m >= 1:
                    pl.wait("sync", vS, t_yw[m - 1][0])
                t_l = pl.dma("sync", qX3D, x3_t[:, buf, :], d_x3[P * m:P * (m + 1), :])
                if m >= 2:
                    pl.wait("tensor", vS, t_yw[m - 2][0])
                for j in range(H1C):
                    if m == 0 and j == 0:
                        pl.wait("tensor", sS, t_hq)
                        pl.wait("tensor", qWDL, t_wd)
                    pl.op("tensor", tS, lambda E, j=j, m=m, pb0=pb0: E.matmul(
                        p_d1[pb0][:, :], h1_fm[:, j, P * m:P * (m + 1)], wd_sb[:, j, 0:512],
                        start=(j == 0), stop=(j == H1C - 1)))
                    t_dm = pl.op("tensor", tS, lambda E, j=j, m=m, pb1=pb1: E.matmul(
                        p_d1[pb1][:, :], h1_fm[:, j, P * m:P * (m + 1)], wd_sb[:, j, 512:1024],
                        start=(j == 0), stop=(j == H1C - 1)))
                pl.wait("vector", tS, t_dm)
                pl.wait("vector", qX3D, t_l)
                if m >= 1:
                    pl.wait("vector", qYO, t_yw[m - 1][1])
                pl.op("vector", vS, lambda E, pb0=pb0, buf=buf: E.scalar_tensor_tensor(
                    y_st[:, buf, 0:512], p_d1[pb0][:, :], shb[:, 0:1], x3_t[:, buf, 0:512],
                    ALU.mult, ALU.add))
                t_ev = pl.op("vector", vS, lambda E, pb1=pb1, buf=buf: E.scalar_tensor_tensor(
                    y_st[:, buf, 512:1024], p_d1[pb1][:, :], shb[:, 0:1], x3_t[:, buf, 512:1024],
                    ALU.mult, ALU.add))
                pl.wait("gpsimd", vS, t_ev)
                t_yw.append((t_ev, pl.dma("gpsimd", qYO, y_out[P * m:P * (m + 1), :], y_st[:, buf, :])))
            if dump:
                pl.dma("gpsimd", qDP, tp_v, d_v_in.ap())
                pl.dma("gpsimd", qDP, tp_y, d_y_in.ap())
                pl.dma("gpsimd", qDP, tp_x3, d_x3)
                pl.dma("gpsimd", qDP, tp_sc[0:1, 0:8], scal_sb[0:1, 0:8])
                t_dump = pl.dma("gpsimd", qDP, tp_sc[0:1, 8:16], mx_sb[0:1, 0:8])
                pl.wait("gpsimd", qDP, t_dump)
            pl.wait("gpsimd", qYO, t_yw[-1][1])

        pl.check()

        block = ent(nc.Block())
        block.sync(lambda E: [f(E) for f in pl.ops["sync"]])
        block.scalar(lambda E: [f(E) for f in pl.ops["scalar"]])
        block.vector(lambda E: [f(E) for f in pl.ops["vector"]])
        block.tensor(lambda E: [f(E) for f in pl.ops["tensor"]])
        block.gpsimd(lambda E: [f(E) for f in pl.ops["gpsimd"]])

    return nc


def _host_inputs(L0_attn_x, L0_ffn_w_gate, L0_ffn_w_up, L0_ffn_w_down, L1_attn_U,
                 L1_ffn_clip_x, L1_ffn_clip_down, L1_ffn_w_gate, L1_ffn_w_up, L1_ffn_w_down):
    bf = ml_dtypes.bfloat16
    x = np.ascontiguousarray(np.asarray(L0_attn_x, dtype=np.float32).reshape(NCORE * TOK, D))

    def pack_lhsT(wT, hc):
        return np.ascontiguousarray(
            wT.reshape(DC, P, hc, P).transpose(2, 1, 0, 3).reshape(hc, P, DC * P))

    w0g_p = pack_lhsT(np.asarray(L0_ffn_w_gate, dtype=np.int32).T, H0C)
    w0u_p = pack_lhsT(np.asarray(L0_ffn_w_up, dtype=np.int32).T, H0C)
    w0d_p = np.ascontiguousarray(np.asarray(L0_ffn_w_down, dtype=np.int32).T.reshape(H0C, P, D))
    w1g = np.ascontiguousarray(np.asarray(L1_ffn_w_gate, dtype=np.float32))
    w1u = np.ascontiguousarray(np.asarray(L1_ffn_w_up, dtype=np.float32))
    w1d = np.ascontiguousarray(np.asarray(L1_ffn_w_down, dtype=np.float32))
    U = np.ascontiguousarray(np.asarray(L1_attn_U, dtype=np.float32).reshape(DC, P, R))
    clip2 = np.stack([np.asarray(L1_ffn_clip_x, dtype=np.float32).reshape(()),
                      np.asarray(L1_ffn_clip_down, dtype=np.float32).reshape(())]).reshape(1, 2)

    trilT = np.triu(np.ones((P, P), dtype=np.float32)).astype(bf)
    sel16 = np.zeros((P, NT, NT), dtype=np.float32)
    for k in range(NT):
        sel16[:, k, k] = 1.0
    sel16 = sel16.reshape(P, NT * NT).astype(bf)
    maskLT = (np.arange(NT)[:, None] < np.arange(NT)[None, :]).astype(np.float32).astype(bf)
    onespack = np.ones((P, P), dtype=np.float32).astype(bf)
    onesf = np.ones((1, P), dtype=np.float32)

    in_maps = []
    for c in range(NCORE):
        cidm = np.broadcast_to((np.arange(NCORE)[:, None] < c), (NCORE, 32)).astype(np.float32).astype(bf)
        tglob = TOK * c + P * np.arange(NT)[None, :] + np.arange(P)[:, None]
        invcnt = (1.0 / (tglob + 1.0)).astype(np.float32)
        in_maps.append({
            "x": x[TOK * c:TOK * (c + 1)],
            "w0g": w0g_p, "w0u": w0u_p, "w0d": w0d_p,
            "w1g": w1g, "w1u": w1u, "w1d": w1d,
            "uproj": U, "clip2": clip2,
            "trilT": trilT, "sel16": sel16, "maskLT": maskLT,
            "cidm": cidm, "invcnt": invcnt,
            "onespack": onespack, "onesf": onesf,
        })
    return in_maps


def kernel(L0_attn_x, L0_ffn_w_gate, L0_ffn_w_up, L0_ffn_w_down, L1_attn_U,
           L1_ffn_clip_x, L1_ffn_clip_down, L1_ffn_w_gate, L1_ffn_w_up, L1_ffn_w_down):
    in_maps = _host_inputs(L0_attn_x, L0_ffn_w_gate, L0_ffn_w_up, L0_ffn_w_down, L1_attn_U,
                           L1_ffn_clip_x, L1_ffn_clip_down, L1_ffn_w_gate, L1_ffn_w_up,
                           L1_ffn_w_down)
    nc = build_program()
    res = run_bass_kernel_spmd(nc, in_maps, core_ids=list(range(NCORE)))
    out = np.concatenate([np.asarray(r["y"]) for r in res.results], axis=0)
    return out.reshape(1, NCORE * TOK, D).astype(np.float32)


# version 09
# speedup vs baseline: 1.0289x; 1.0289x over previous
import numpy as np
import ml_dtypes
from contextlib import ExitStack

import concourse.bass as bass
import concourse.mybir as mybir
from concourse.bass_utils import run_bass_kernel_spmd

F32 = mybir.dt.float32
BF16 = mybir.dt.bfloat16
I32 = mybir.dt.int32
AF = mybir.ActivationFunctionType
ALU = mybir.AluOpType
AX = mybir.AxisListType

P = 128
NT = 16
TOK = 2048
D = 1024
DC = 8
H0, H0C = 3072, 24
H1, H1C = 2816, 22
R = 32
NCORE = 8
QG_GROUPS = [(0, 3), (384, 3), (768, 3), (1152, 3), (1536, 2), (1792, 2)]
EPS = 1e-5
INV127 = 1.0 / 127.0
INV127SQ = 1.0 / 16129.0
ENGS = ("sync", "scalar", "vector", "tensor", "gpsimd")


class Ctr:
    def __init__(self, handle, name):
        self.h = handle
        self.name = name
        self.v = 0


class Plan:
    def __init__(self, nc):
        self.nc = nc
        self.ops = {k: [] for k in ENGS}
        self.meta = {k: [] for k in ENGS}

    def wait(self, eng, ctr, tgt):
        if tgt <= 0:
            return
        self.ops[eng].append(lambda E, h=ctr.h, t=tgt: E.wait_ge(h, t))
        self.meta[eng].append(("wait", ctr.name, tgt))

    def _inc(self, eng, ctr, n):
        ctr.v += n
        self.meta[eng].append(("inc", ctr.name, ctr.v))
        return ctr.v

    def dma(self, eng, ctr, out, in_, **kw):
        self.ops[eng].append(
            lambda E, o=out, i=in_, k=kw, h=ctr.h: E.dma_start(out=o, in_=i, **k).then_inc(h, 16))
        return self._inc(eng, ctr, 16)

    def dmat(self, eng, ctr, out, in_):
        self.ops[eng].append(
            lambda E, o=out, i=in_, h=ctr.h: E.dma_start_transpose(o, i).then_inc(h, 16))
        return self._inc(eng, ctr, 16)

    def op(self, eng, ctr, fn):
        self.ops[eng].append(lambda E, f=fn, h=ctr.h: f(E).then_inc(h, 1))
        return self._inc(eng, ctr, 1)

    def eng_inc(self, eng, ctr):
        self.ops[eng].append(lambda E, h=ctr.h: E.sem_inc(h, 1))
        return self._inc(eng, ctr, 1)

    def check(self):
        n_ops = {e: len(self.meta[e]) for e in ENGS}
        done = {e: 0 for e in ENGS}
        sem = {}
        progress, ran, total = True, 0, sum(n_ops.values())
        while progress:
            progress = False
            for e in ENGS:
                while done[e] < n_ops[e]:
                    m = self.meta[e][done[e]]
                    if m[0] == "wait":
                        if sem.get(m[1], 0) < m[2]:
                            break
                    else:
                        sem[m[1]] = max(sem.get(m[1], 0), m[2])
                    done[e] += 1
                    ran += 1
                    progress = True
        if ran != total:
            stuck = {e: (done[e], n_ops[e], self.meta[e][done[e]] if done[e] < n_ops[e] else None)
                     for e in ENGS if done[e] < n_ops[e]}
            raise RuntimeError(f"plan deadlock: ran {ran}/{total}; stuck at {stuck}")


def build_program(dump=False):
    nc = bass.Bass("TRN2", target_bir_lowering=False, debug=False, detect_race_conditions=False)
    ap = lambda t: t.ap()

    x_in = ap(nc.dram_tensor("x", [TOK, D], F32, kind="ExternalInput"))
    w0g_in = ap(nc.dram_tensor("w0g", [H0C, P, D], I32, kind="ExternalInput"))
    w0u_in = ap(nc.dram_tensor("w0u", [H0C, P, D], I32, kind="ExternalInput"))
    w0d_in = ap(nc.dram_tensor("w0d", [H0C, P, D], I32, kind="ExternalInput"))
    w1g_in = ap(nc.dram_tensor("w1g", [H1, D], F32, kind="ExternalInput"))
    w1u_in = ap(nc.dram_tensor("w1u", [H1, D], F32, kind="ExternalInput"))
    w1d_in = ap(nc.dram_tensor("w1d", [D, H1], F32, kind="ExternalInput"))
    u_in = ap(nc.dram_tensor("uproj", [DC, P, R], F32, kind="ExternalInput"))
    clip_in = ap(nc.dram_tensor("clip2", [1, 2], F32, kind="ExternalInput"))
    trilT_in = ap(nc.dram_tensor("trilT", [P, P], BF16, kind="ExternalInput"))
    sel16_in = ap(nc.dram_tensor("sel16", [P, NT * NT], BF16, kind="ExternalInput"))
    maskLT_in = ap(nc.dram_tensor("maskLT", [NT, NT], BF16, kind="ExternalInput"))
    cidm_in = ap(nc.dram_tensor("cidm", [NCORE, 32], BF16, kind="ExternalInput"))
    invcnt_in = ap(nc.dram_tensor("invcnt", [P, NT], F32, kind="ExternalInput"))
    ones_in = ap(nc.dram_tensor("onespack", [P, P], BF16, kind="ExternalInput"))
    onesf_in = ap(nc.dram_tensor("onesf", [1, P], F32, kind="ExternalInput"))

    y_out = ap(nc.dram_tensor("y", [TOK, D], F32, kind="ExternalOutput"))

    d_bs_in = nc.dram_tensor("d_bs_in", [1, D], BF16)
    d_bs_out = nc.dram_tensor("d_bs_out", [NCORE, D], BF16, addr_space="Shared")
    d_v_in = nc.dram_tensor("d_v_in", [TOK, D], BF16)
    d_vgat = nc.dram_tensor("d_vgat", [NCORE * TOK, D], BF16, addr_space="Shared")
    d_y_in = nc.dram_tensor("d_y_in", [R + 2, TOK], BF16)
    d_y_out = nc.dram_tensor("d_y_out", [NCORE * (R + 2), TOK], BF16, addr_space="Shared")
    d_mx1_in = nc.dram_tensor("d_mx1_in", [1, 8], F32)
    d_mx1_out = nc.dram_tensor("d_mx1_out", [1, 8], F32)
    d_mx2_in = nc.dram_tensor("d_mx2_in", [1, 8], F32)
    d_mx2_out = nc.dram_tensor("d_mx2_out", [1, 8], F32)
    d_x3 = ap(nc.dram_tensor("d_x3", [TOK, D], F32))
    d_wgq = ap(nc.dram_tensor("d_wgq", [H1C, P, D], BF16))
    d_wuq = ap(nc.dram_tensor("d_wuq", [H1C, P, D], BF16))
    d_wdq = ap(nc.dram_tensor("d_wdq", [H1C, P, D], BF16))
    if dump:
        tp_x1 = ap(nc.dram_tensor("tp_x1", [TOK, D], F32, kind="ExternalOutput"))
        tp_x2 = ap(nc.dram_tensor("tp_x2", [TOK, D], F32, kind="ExternalOutput"))
        tp_v = ap(nc.dram_tensor("tp_v", [TOK, D], F32, kind="ExternalOutput"))
        tp_y = ap(nc.dram_tensor("tp_y", [R + 2, TOK], F32, kind="ExternalOutput"))
        tp_x3 = ap(nc.dram_tensor("tp_x3", [TOK, D], F32, kind="ExternalOutput"))
        tp_sc = ap(nc.dram_tensor("tp_sc", [1, 16], F32, kind="ExternalOutput"))
        tp_ln0 = ap(nc.dram_tensor("tp_ln0", [TOK, D], F32, kind="ExternalOutput"))
        tp_cc = ap(nc.dram_tensor("tp_cc", [2 * NT + 2, D], F32, kind="ExternalOutput"))

    pl = Plan(nc)

    with ExitStack() as ctx:
        ent = ctx.enter_context

        def sem(name):
            return Ctr(ent(nc.semaphore(name)), name)

        tS, vS, sS, gS = sem("tS"), sem("vS"), sem("sS"), sem("gS")
        qX0 = sem("qX0")
        qX0B = sem("qX0B")
        qCN = sem("qCN")
        qBS1, qBS2 = sem("qBS1"), sem("qBS2")
        qCF = sem("qCF")
        qWG0, qWU0, qWD0 = sem("qWG0"), sem("qWU0"), sem("qWD0")
        qT1 = sem("qT1")
        qTV = sem("qTV")
        qVS = sem("qVS")
        qYS = sem("qYS")
        qYL = sem("qYL")
        qNQ = sem("qNQ")
        qVL = sem("qVL")
        qWJ = sem("qWJ")
        qWT = sem("qWT")
        qSP = sem("qSP")
        qX3W = sem("qX3W")
        qX3L = sem("qX3L")
        qX3D = sem("qX3D")
        qMW, qHW = sem("qMW"), sem("qHW")
        qM1a, qM1b, qM2a, qM2b = sem("qM1a"), sem("qM1b"), sem("qM2a"), sem("qM2b")
        qWQL = sem("qWQL")
        qWDL = sem("qWDL")
        qTX = sem("qTX")
        qYO = sem("qYO")
        cBS, cV, cY, cM1, cM2 = sem("cBS"), sem("cV"), sem("cY"), sem("cM1"), sem("cM2")
        if dump:
            qDP = sem("qDP")

        c_trilT = ent(nc.sbuf_tensor("c_trilT", [P, P], BF16))
        c_sel16 = ent(nc.sbuf_tensor("c_sel16", [P, NT * NT], BF16))
        c_maskLT = ent(nc.sbuf_tensor("c_maskLT", [NT, NT], BF16))
        c_cidm = ent(nc.sbuf_tensor("c_cidm", [NCORE, 32], BF16))
        c_invcnt = ent(nc.sbuf_tensor("c_invcnt", [P, NT], F32))
        c_ones = ent(nc.sbuf_tensor("c_ones", [P, P], BF16))
        c_onesf = ent(nc.sbuf_tensor("c_onesf", [1, P], F32))
        sq_f32 = ent(nc.sbuf_tensor("sq_f32", [P, 2, D], F32))
        st_sq = ent(nc.sbuf_tensor("st_sq", [P, 16], F32))
        st_w = ent(nc.sbuf_tensor("st_w", [P, 10, 16], F32))
        mxacc = ent(nc.sbuf_tensor("mxacc", [P, 32], F32))
        st_sum = ent(nc.sbuf_tensor("st_sum", [P, 16], F32))
        sclip = ent(nc.sbuf_tensor("sclip", [1, 2], F32))
        uU = ent(nc.sbuf_tensor("uU", [P, DC, R], F32))
        uUb = ent(nc.sbuf_tensor("uUb", [P, DC, R], BF16))
        sxb = ent(nc.sbuf_tensor("sxb", [P, 16], F32))
        shb = ent(nc.sbuf_tensor("shb", [P, 16], F32))
        mx_sb = ent(nc.sbuf_tensor("mx_sb", [1, 16], F32))
        mxh_sb = ent(nc.sbuf_tensor("mxh_sb", [1, 16], F32))
        scal_sb = ent(nc.sbuf_tensor("scal_sb", [1, 16], F32))

        s123 = ExitStack()
        xres = s123.enter_context(nc.sbuf_tensor("xres", [P, NT, D], F32))

        t_x0h = pl.dma("sync", qX0, xres[:, 0:8, :], x_in[0:8 * P, :].rearrange("(m p) d -> p m d", p=P))
        t_x0 = pl.dma("sync", qX0B, xres[:, 8:16, :], x_in[8 * P:16 * P, :].rearrange("(m p) d -> p m d", p=P))
        pl.dma("scalar", qCN, c_trilT[:], trilT_in)
        pl.dma("scalar", qCN, c_sel16[:], sel16_in)
        pl.dma("scalar", qCN, c_maskLT[:], maskLT_in)
        pl.dma("scalar", qCN, c_cidm[:], cidm_in)
        pl.dma("scalar", qCN, c_invcnt[:], invcnt_in)
        pl.dma("scalar", qCN, c_ones[:], ones_in)
        pl.dma("scalar", qCN, c_onesf[:], onesf_in)
        pl.dma("scalar", qCN, sclip[:], clip_in)
        t_consts = pl.dma("scalar", qCN, uU[:], u_in.rearrange("c p r -> p c r"))

        def emit_ln_batch(tiles, dst_pre=None, post_dst=None, col_base=0):
            n = len(tiles)
            cb = col_base
            t_stats = []
            for k, (s, _d, vw) in enumerate(tiles):
                for ctr, tgt in vw:
                    pl.wait("scalar", ctr, tgt)
                if k >= 2:
                    pl.wait("scalar", vS, t_stats[k - 2])
                t_sq2 = pl.op("scalar", sS, lambda E, s=s, k=k: E.activation(
                    sq_f32[:, k % 2, :], s, AF.Square))
                for ctr, tgt in vw:
                    pl.wait("vector", ctr, tgt)
                pl.op("vector", vS, lambda E, s=s, k=k: E.reduce_sum(
                    st_sum[:, cb + k:cb + k + 1], s, axis=AX.X))
                pl.wait("vector", sS, t_sq2)
                t_stats.append(pl.op("vector", vS, lambda E, k=k: E.reduce_sum(
                    st_sq[:, cb + k:cb + k + 1], sq_f32[:, k % 2, :], axis=AX.X)))
            pl.wait("gpsimd", vS, t_stats[-1])
            pl.op("gpsimd", gS, lambda E, n=n: E.tensor_scalar(
                st_w[:, 0, cb:cb + n], st_sum[:, cb:cb + n], 1.0 / D, None, ALU.mult))
            pl.op("gpsimd", gS, lambda E, n=n: E.tensor_scalar(
                st_w[:, 1, cb:cb + n], st_sq[:, cb:cb + n], 1.0 / D, None, ALU.mult))
            pl.op("gpsimd", gS, lambda E, n=n: E.tensor_tensor(
                st_w[:, 2, cb:cb + n], st_w[:, 0, cb:cb + n], st_w[:, 0, cb:cb + n], ALU.mult))
            pl.op("gpsimd", gS, lambda E, n=n: E.tensor_tensor(
                st_w[:, 3, cb:cb + n], st_w[:, 1, cb:cb + n], st_w[:, 2, cb:cb + n], ALU.subtract))
            t_gv = pl.op("gpsimd", gS, lambda E, n=n: E.tensor_scalar(
                st_w[:, 4, cb:cb + n], st_w[:, 3, cb:cb + n], EPS, None, ALU.add))
            pl.wait("scalar", gS, t_gv)
            t_sd = pl.op("scalar", sS, lambda E, n=n: E.activation(
                st_w[:, 5, cb:cb + n], st_w[:, 4, cb:cb + n], AF.Sqrt))
            pl.wait("vector", sS, t_sd)
            pl.op("vector", vS, lambda E, n=n: E.reciprocal(st_w[:, 6, cb:cb + n], st_w[:, 5, cb:cb + n]))
            t_dsts = []
            for k, (s, d, _vw) in enumerate(tiles):
                if dst_pre is not None:
                    dst_pre(k)
                t_dst = pl.op("vector", vS, lambda E, s=s, d=d, k=k: E.tensor_scalar(
                    d, s, st_w[:, 0, cb + k:cb + k + 1], st_w[:, 6, cb + k:cb + k + 1],
                    ALU.subtract, ALU.mult))
                t_dsts.append(t_dst)
                if post_dst is not None:
                    post_dst(k, t_dst)
            return t_dsts, t_stats

        t_prefix_evac = []
        with ExitStack() as ph:
            pe = ph.enter_context
            ln0 = pe(nc.sbuf_tensor("ln0", [P, NT, D], BF16))
            colsums = pe(nc.sbuf_tensor("colsums", [NT, D], BF16))
            bs_row = pe(nc.sbuf_tensor("bs_row", [1, D], BF16))
            gath_bs = pe(nc.sbuf_tensor("gath_bs", [NCORE, D], BF16))
            carry0 = pe(nc.sbuf_tensor("carry0", [1, D], BF16))
            carries = pe(nc.sbuf_tensor("carries", [NT, D], BF16))
            carries_f = pe(nc.sbuf_tensor("carries_f", [1, NT, D], BF16))

            t_ln0, _ = emit_ln_batch(
                [(xres[:, m, :], ln0[:, m, :],
                  [(qX0, t_x0h)] if m == 0 else ([(qX0B, t_x0)] if m == 8 else []))
                 for m in range(NT)])

            with ExitStack() as pp:
                p_cs0 = pp.enter_context(nc.psum_tensor("p_cs0", [NT, 512], F32))
                p_cs1 = pp.enter_context(nc.psum_tensor("p_cs1", [NT, 512], F32))
                pl.wait("tensor", qCN, t_consts)
                for m in range(NT):
                    pl.wait("tensor", vS, t_ln0[m])
                    pl.op("tensor", tS, lambda E, m=m: E.matmul(
                        p_cs0[:, :], c_sel16[:, NT * m:NT * (m + 1)], ln0[:, m, 0:512],
                        start=(m == 0), stop=(m == NT - 1)))
                    t_csm = pl.op("tensor", tS, lambda E, m=m: E.matmul(
                        p_cs1[:, :], c_sel16[:, NT * m:NT * (m + 1)], ln0[:, m, 512:1024],
                        start=(m == 0), stop=(m == NT - 1)))
                pl.wait("vector", tS, t_csm)
                pl.op("vector", vS, lambda E: E.tensor_copy(colsums[:, 0:512], p_cs0[:, :]))
                t_colsums = pl.op("vector", vS, lambda E: E.tensor_copy(colsums[:, 512:1024], p_cs1[:, :]))

            with ExitStack() as pp:
                p_bs0 = pp.enter_context(nc.psum_tensor("p_bs0", [1, 512], F32))
                p_bs1 = pp.enter_context(nc.psum_tensor("p_bs1", [1, 512], F32))
                p_c0a = pp.enter_context(nc.psum_tensor("p_c0a", [1, 512], F32))
                p_c0b = pp.enter_context(nc.psum_tensor("p_c0b", [1, 512], F32))
                pl.wait("tensor", vS, t_colsums)
                pl.op("tensor", tS, lambda E: E.matmul(
                    p_bs0[:, :], c_ones[0:NT, 0:1], colsums[:, 0:512], start=True, stop=True))
                t_bsm = pl.op("tensor", tS, lambda E: E.matmul(
                    p_bs1[:, :], c_ones[0:NT, 0:1], colsums[:, 512:1024], start=True, stop=True))
                pl.wait("vector", tS, t_bsm)
                pl.op("vector", vS, lambda E: E.tensor_copy(bs_row[:, 0:512], p_bs0[:, :]))
                t_bsrow = pl.op("vector", vS, lambda E: E.tensor_copy(bs_row[:, 512:1024], p_bs1[:, :]))
                pl.wait("gpsimd", vS, t_bsrow)
                t_bst = pl.dma("gpsimd", qBS1, d_bs_in.ap(), bs_row[:])
                pl.wait("gpsimd", qBS1, t_bst)
                t_ag_bs = pl.op("gpsimd", cBS, lambda E: E.collective_compute(
                    "AllGather", ALU.bypass, replica_groups=[list(range(NCORE))],
                    ins=[d_bs_in.ap().opt()], outs=[d_bs_out.ap().opt()]))
                pl.wait("gpsimd", cBS, t_ag_bs)
                t_gbs = pl.dma("gpsimd", qBS2, gath_bs[:], d_bs_out.ap())
                pl.wait("tensor", qBS2, t_gbs)
                pl.op("tensor", tS, lambda E: E.matmul(
                    p_c0a[:, :], c_cidm[:, 0:1], gath_bs[:, 0:512], start=True, stop=True))
                t_c0m = pl.op("tensor", tS, lambda E: E.matmul(
                    p_c0b[:, :], c_cidm[:, 0:1], gath_bs[:, 512:1024], start=True, stop=True))
                pl.wait("vector", tS, t_c0m)
                pl.op("vector", vS, lambda E: E.tensor_copy(carry0[:, 0:512], p_c0a[:, :]))
                t_c0 = pl.op("vector", vS, lambda E: E.tensor_copy(carry0[:, 512:1024], p_c0b[:, :]))

            with ExitStack() as pp:
                p_ca0 = pp.enter_context(nc.psum_tensor("p_ca0", [NT, 512], F32))
                p_ca1 = pp.enter_context(nc.psum_tensor("p_ca1", [NT, 512], F32))
                pl.wait("tensor", vS, t_c0)
                pl.op("tensor", tS, lambda E: E.matmul(
                    p_ca0[:, :], c_maskLT[:], colsums[:, 0:512], start=True, stop=False))
                pl.op("tensor", tS, lambda E: E.matmul(
                    p_ca0[:, :], c_ones[0:1, 0:NT], carry0[:, 0:512], start=False, stop=True))
                pl.op("tensor", tS, lambda E: E.matmul(
                    p_ca1[:, :], c_maskLT[:], colsums[:, 512:1024], start=True, stop=False))
                t_cam = pl.op("tensor", tS, lambda E: E.matmul(
                    p_ca1[:, :], c_ones[0:1, 0:NT], carry0[:, 512:1024], start=False, stop=True))
                pl.wait("vector", tS, t_cam)
                pl.op("vector", vS, lambda E: E.tensor_copy(carries[:, 0:512], p_ca0[:, :]))
                t_car = pl.op("vector", vS, lambda E: E.tensor_copy(carries[:, 512:1024], p_ca1[:, :]))
                pl.wait("scalar", vS, t_car)
                t_carf = pl.dma("scalar", qCF, carries_f[0:1, :, :].rearrange("a m d -> a (m d)"), carries[:])

            with ExitStack() as pp:
                p_pr0 = pp.enter_context(nc.psum_tensor("p_pr0", [P, 512], F32))
                p_pr1 = pp.enter_context(nc.psum_tensor("p_pr1", [P, 512], F32))
                pl.wait("tensor", qCF, t_carf)
                for m in range(NT):
                    if m >= 1:
                        pl.wait("tensor", vS, t_prefix_evac[m - 1])
                    pl.op("tensor", tS, lambda E, m=m: E.matmul(
                        p_pr0[:, :], c_trilT[:], ln0[:, m, 0:512], start=True, stop=False))
                    pl.op("tensor", tS, lambda E, m=m: E.matmul(
                        p_pr0[:, :], c_ones[0:1, 0:P], carries_f[0:1, m, 0:512], start=False, stop=True))
                    pl.op("tensor", tS, lambda E, m=m: E.matmul(
                        p_pr1[:, :], c_trilT[:], ln0[:, m, 512:1024], start=True, stop=False))
                    t_pm = pl.op("tensor", tS, lambda E, m=m: E.matmul(
                        p_pr1[:, :], c_ones[0:1, 0:P], carries_f[0:1, m, 512:1024], start=False, stop=True))
                    pl.wait("vector", tS, t_pm)
                    pl.op("vector", vS, lambda E, m=m: E.scalar_tensor_tensor(
                        xres[:, m, 0:512], p_pr0[:, :], c_invcnt[:, m:m + 1], xres[:, m, 0:512],
                        ALU.mult, ALU.add))
                    t_prefix_evac.append(pl.op("vector", vS, lambda E, m=m: E.scalar_tensor_tensor(
                        xres[:, m, 512:1024], p_pr1[:, :], c_invcnt[:, m:m + 1], xres[:, m, 512:1024],
                        ALU.mult, ALU.add)))
                t_l0a_tensor = t_pm
            if dump:
                t_tap1 = []
                for m in range(NT):
                    pl.wait("gpsimd", vS, t_prefix_evac[m])
                    t_tap1.append(pl.dma("gpsimd", qDP, tp_x1[P * m:P * (m + 1), :], xres[:, m, :]))
                pl.dma("gpsimd", qDP, tp_ln0, ln0[:].rearrange("p m d -> p (m d)"))
                pl.dma("gpsimd", qDP, tp_cc[0:NT, :], colsums[:])
                pl.dma("gpsimd", qDP, tp_cc[NT:2 * NT, :], carries[:])
                pl.dma("gpsimd", qDP, tp_cc[2 * NT:2 * NT + 1, :], carry0[:])
                t_tap1.append(pl.dma("gpsimd", qDP, tp_cc[2 * NT + 1:2 * NT + 2, :], bs_row[:]))

        t_x2 = []
        with ExitStack() as ph:
            pe = ph.enter_context
            ln1_fm = pe(nc.sbuf_tensor("ln1_fm", [P, 8, DC, P], BF16))
            ln1_tm = pe(nc.sbuf_tensor("ln1_tm", [P, 2, D], BF16))
            h0_fm = pe(nc.sbuf_tensor("h0_fm", [P, H0C, 1024], BF16))
            w0g_sb = pe(nc.sbuf_tensor("w0g_sb", [P, 2, D], BF16))
            w0u_sb = pe(nc.sbuf_tensor("w0u_sb", [P, 2, D], BF16))
            w0d_sb = pe(nc.sbuf_tensor("w0d_sb", [P, H0C, D], BF16))
            silu_sb = pe(nc.sbuf_tensor("silu_sb", [P, 2, 512], BF16))
            p_g = [pe(nc.psum_tensor("p_g%d" % b, [P, 512], F32)) for b in range(2)]
            p_u = [pe(nc.psum_tensor("p_u%d" % b, [P, 512], F32)) for b in range(2)]
            p_d = [pe(nc.psum_tensor("p_d%d" % b, [P, 512], F32)) for b in range(4)]

            t_w0d = 0
            for j in range(H0C):
                if j == 0:
                    pl.wait("gpsimd", tS, t_l0a_tensor)
                t_w0d = pl.dma("gpsimd", qWD0, w0d_sb[:, j, :], w0d_in[j])

            t_ln1tr = {}
            t_gu_first = {}
            t_gu_mm = {}
            t_hev = {}
            t_down_last = {}

            def emit_ln1_half(half):
                def _ln1_pre(k):
                    m = 8 * half + k
                    if m >= 1:
                        pl.wait("vector", qT1, t_ln1tr[m - 1])
                    if m == 0:
                        pl.wait("vector", tS, t_l0a_tensor)

                def _ln1_post(k, t_d):
                    m = 8 * half + k
                    pl.wait("sync", vS, t_d)
                    if half == 1 and k == 0:
                        pl.wait("sync", tS, t_gu_mm[(0, H0C - 1)])
                    t_ln1tr[m] = pl.dmat("sync", qT1, ln1_fm[:, k, :, :], ln1_tm[:, m % 2, :])

                emit_ln_batch(
                    [(xres[:, 8 * half + k, :], ln1_tm[:, (8 * half + k) % 2, :],
                      [(vS, t_prefix_evac[8 * half + k])]) for k in range(8)],
                    dst_pre=_ln1_pre, post_dst=_ln1_post)

            def emit_gu_half(half):
                nonlocal_state = {}
                for i in range(H0C):
                    ib = i % 2
                    prevc = (half, i - 1) if i >= 1 else ((0, H0C - 1) if half else None)
                    if prevc is not None:
                        pl.wait("gpsimd", tS, t_gu_first[prevc])
                    tg_ = pl.dma("gpsimd", qWG0, w0g_sb[:, ib, :], w0g_in[i])
                    tu_ = pl.dma("gpsimd", qWU0, w0u_sb[:, ib, :], w0u_in[i])
                    pl.wait("tensor", qWG0, tg_)
                    pl.wait("tensor", qWU0, tu_)
                    if i == 0:
                        pl.wait("tensor", qT1, t_ln1tr[8 * half + 7])
                        if half == 0:
                            pl.wait("tensor", vS, t_prefix_evac[-1])
                    first_of_chunk = True
                    for tg in range(2):
                        slot = 2 * i + tg
                        pb = slot % 2
                        prevs = (half, (slot - 2) // 2, (slot - 2) % 2) if slot >= 2 else \
                                ((half - 1, (2 * H0C + slot - 2) // 2, (2 * H0C + slot - 2) % 2) if half else None)
                        if prevs is not None:
                            pl.wait("tensor", vS, t_hev[prevs])
                        for c in range(DC):
                            t_mmg = pl.op("tensor", tS, lambda E, tg=tg, c=c, ib=ib, pb=pb: E.matmul(
                                p_g[pb][:, :], w0g_sb[:, ib, P * c:P * (c + 1)],
                                ln1_fm[:, 4 * tg:4 * tg + 4, c, :],
                                start=(c == 0), stop=(c == DC - 1)))
                            if first_of_chunk:
                                t_gu_first[(half, i)] = t_mmg
                                first_of_chunk = False
                        for c in range(DC):
                            t_mmu = pl.op("tensor", tS, lambda E, tg=tg, c=c, ib=ib, pb=pb: E.matmul(
                                p_u[pb][:, :], w0u_sb[:, ib, P * c:P * (c + 1)],
                                ln1_fm[:, 4 * tg:4 * tg + 4, c, :],
                                start=(c == 0), stop=(c == DC - 1)))
                        pl.wait("scalar", tS, t_mmu)
                        t_si = pl.op("scalar", sS, lambda E, pb=pb: E.activation(
                            silu_sb[:, pb, :], p_g[pb][:, :], AF.Silu, scale=INV127))
                        pl.wait("vector", sS, t_si)
                        if half == 1 and slot == 0:
                            pl.wait("vector", tS, t_down_last[0])
                        t_hev[(half, i, tg)] = pl.op("vector", vS, lambda E, i=i, tg=tg, pb=pb: E.tensor_tensor(
                            h0_fm[:, i, 512 * tg:512 * (tg + 1)], silu_sb[:, pb, :], p_u[pb][:, :], ALU.mult))
                    t_gu_mm[(half, i)] = t_mmu

            def emit_down_half(half):
                t_dm = 0
                for mt in range(8):
                    m = 8 * half + mt
                    pb0, pb1 = 2 * (mt % 2), 2 * (mt % 2) + 1
                    if m >= 2:
                        pl.wait("tensor", vS, t_x2[m - 2])
                    for j in range(H0C):
                        if mt == 0 and j == 0:
                            pl.wait("tensor", vS, t_hev[(half, H0C - 1, 1)])
                            pl.wait("tensor", qWD0, t_w0d)
                        pl.op("tensor", tS, lambda E, j=j, mt=mt, pb0=pb0: E.matmul(
                            p_d[pb0][:, :], h0_fm[:, j, P * mt:P * (mt + 1)], w0d_sb[:, j, 0:512],
                            start=(j == 0), stop=(j == H0C - 1)))
                        t_dm = pl.op("tensor", tS, lambda E, j=j, mt=mt, pb1=pb1: E.matmul(
                            p_d[pb1][:, :], h0_fm[:, j, P * mt:P * (mt + 1)], w0d_sb[:, j, 512:1024],
                            start=(j == 0), stop=(j == H0C - 1)))
                    pl.wait("vector", tS, t_dm)
                    if dump:
                        pl.wait("vector", qDP, t_tap1[-1])
                    pl.op("vector", vS, lambda E, m=m, pb0=pb0: E.scalar_tensor_tensor(
                        xres[:, m, 0:512], p_d[pb0][:, :], INV127SQ, xres[:, m, 0:512], ALU.mult, ALU.add))
                    t_x2.append(pl.op("vector", vS, lambda E, m=m, pb1=pb1: E.scalar_tensor_tensor(
                        xres[:, m, 512:1024], p_d[pb1][:, :], INV127SQ, xres[:, m, 512:1024],
                        ALU.mult, ALU.add)))
                    if dump:
                        pl.wait("gpsimd", vS, t_x2[-1])
                        pl.dma("gpsimd", qDP, tp_x2[P * m:P * (m + 1), :], xres[:, m, :])
                t_down_last[half] = t_dm
                return t_dm

            emit_ln1_half(0)
            emit_gu_half(0)
            emit_ln1_half(1)
            emit_down_half(0)
            emit_gu_half(1)
            t_dm = emit_down_half(1)
            t_l0f_tensor = t_dm

        ph3arena = ExitStack()
        pe34 = ph3arena.enter_context
        yq_ext = pe34(nc.sbuf_tensor("yq_ext", [R + 2, TOK], BF16))
        yT_all = pe34(nc.sbuf_tensor("yT_all", [R + 2, NCORE, TOK], BF16))

        t_y_frontier = None
        with ExitStack() as ph:
            pe = ph.enter_context
            v_tm = pe(nc.sbuf_tensor("v_tm", [P, 2, D], BF16))
            v_fm = pe(nc.sbuf_tensor("v_fm", [P, DC, TOK], BF16))
            yT_f = pe(nc.sbuf_tensor("yT_f", [R, TOK], F32))
            y2_b = pe(nc.sbuf_tensor("y2_b", [R, TOK], BF16))
            nrow = pe(nc.sbuf_tensor("nrow", [1, TOK], BF16))

            t_vtr = {}
            vstate = {"stage": 0}

            def _v_pre(k):
                if k >= 1:
                    pl.wait("vector", qVS, 16 * k)
                    pl.wait("vector", qTV, t_vtr[k - 1])
                if k == 0:
                    pl.wait("vector", tS, t_l0f_tensor)
                    pl.wait("sync", tS, t_l0f_tensor)
                    pl.wait("gpsimd", tS, t_l0f_tensor)

            def _v_post(k, t_v):
                pl.wait("gpsimd", vS, t_v)
                vstate["stage"] = pl.dma("gpsimd", qVS, d_v_in.ap()[P * k:P * (k + 1), :],
                                         v_tm[:, k % 2, :])
                pl.wait("sync", vS, t_v)
                t_vtr[k] = pl.dmat("sync", qTV, v_fm[:, :, P * k:P * (k + 1)], v_tm[:, k % 2, :])

            emit_ln_batch([(xres[:, m, :], v_tm[:, m % 2, :], [(vS, t_x2[m])])
                           for m in range(NT)], dst_pre=_v_pre, post_dst=_v_post)
            pl.wait("gpsimd", qVS, vstate["stage"])
            t_ag_v = pl.op("gpsimd", cV, lambda E: E.collective_compute(
                "AllGather", ALU.bypass, replica_groups=[list(range(NCORE))],
                ins=[d_v_in.ap().opt()], outs=[d_vgat.ap().opt()]))

            pl.op("vector", vS, lambda E: E.tensor_copy(
                uUb[:].rearrange("p c r -> p (c r)"), uU[:].rearrange("p c r -> p (c r)")))
            with ExitStack() as pp:
                p_y = pp.enter_context(nc.psum_tensor("p_y", [R, 512], F32))
                p_n = pp.enter_context(nc.psum_tensor("p_n", [1, 512], F32))
                pl.wait("tensor", qTV, t_vtr[NT - 1])
                pl.wait("tensor", qCN, t_consts)
                t_ye = 0
                for tg in range(4):
                    if tg >= 1:
                        pl.wait("tensor", vS, t_ye)
                    for c in range(DC):
                        t_ym = pl.op("tensor", tS, lambda E, tg=tg, c=c: E.matmul(
                            p_y[:, :], uUb[:, c, :], v_fm[:, c, 512 * tg:512 * (tg + 1)],
                            start=(c == 0), stop=(c == DC - 1)))
                    pl.wait("vector", tS, t_ym)
                    t_ye = pl.op("vector", vS, lambda E, tg=tg: E.tensor_copy(
                        yT_f[:, 512 * tg:512 * (tg + 1)], p_y[:, :]))
                pl.op("vector", vS, lambda E: E.tensor_scalar(
                    yq_ext[0:R, :], yT_f[:, :], 2.0, None, ALU.mult))
                t_y2 = pl.op("vector", vS, lambda E: E.tensor_tensor(
                    y2_b[:, :], yT_f[:, :], yT_f[:, :], ALU.mult))
                pl.wait("tensor", vS, t_y2)
                t_ne = 0
                for tg in range(4):
                    if tg >= 1:
                        pl.wait("tensor", vS, t_ne)
                    t_nm = pl.op("tensor", tS, lambda E, tg=tg: E.matmul(
                        p_n[:, :], c_ones[0:R, 0:1], y2_b[:, 512 * tg:512 * (tg + 1)],
                        start=True, stop=True))
                    pl.wait("vector", tS, t_nm)
                    t_ne = pl.op("vector", vS, lambda E, tg=tg: E.tensor_scalar(
                        nrow[:, 512 * tg:512 * (tg + 1)], p_n[:, :], -1.0, None, ALU.mult))
                t_y_frontier = t_nm
            t_ms = pl.op("vector", vS, lambda E: E.memset(yq_ext[R:R + 1, :], 1.0))
            pl.wait("scalar", vS, t_ne)
            t_nq = pl.dma("scalar", qNQ, yq_ext[R + 1:R + 2, :], nrow[:])
            t_yb = pl.op("vector", vS, lambda E: E.tensor_copy(y2_b[:, :], yT_f[:, :]))
            pl.wait("gpsimd", vS, t_yb)
            pl.dma("gpsimd", qYS, d_y_in.ap()[0:R, :], y2_b[:, :])
            pl.wait("gpsimd", vS, t_ne)
            t_n32 = pl.dma("gpsimd", qYS, d_y_in.ap()[R:R + 1, :], nrow[:])
            pl.wait("vector", qYS, t_n32)
            pl.wait("vector", qNQ, t_nq)
            t_on = pl.op("vector", vS, lambda E: E.memset(nrow[:, :], 1.0))
            pl.wait("gpsimd", vS, t_on)
            t_ystage = pl.dma("gpsimd", qYS, d_y_in.ap()[R + 1:R + 2, :], nrow[:])
            pl.wait("gpsimd", qYS, t_ystage)
            t_ag_y = pl.op("gpsimd", cY, lambda E: E.collective_compute(
                "AllGather", ALU.bypass, replica_groups=[list(range(NCORE))],
                ins=[d_y_in.ap().opt()], outs=[d_y_out.ap().opt()]))
            pl.wait("gpsimd", cY, t_ag_y)
            t_yall = 0
            for cb in range(NCORE):
                t_yall = pl.dma("gpsimd", qYL, yT_all[:, cb, :],
                                d_y_out.ap()[(R + 2) * cb:(R + 2) * (cb + 1), :])

        wprep = ExitStack()
        pw = wprep.enter_context
        wf_sb = pw(nc.sbuf_tensor("wf_sb", [P, 2, H1], F32))
        wq_nat = pw(nc.sbuf_tensor("wq_nat", [P, 2, H1], BF16))
        wq_i32 = pw(nc.sbuf_tensor("wq_i32", [P, H1], I32))
        wq_st = pw(nc.sbuf_tensor("wq_st", [P, H1C, P], BF16, is_transpose_dest=True))
        wst = pw(nc.sbuf_tensor("wst", [P, 16], F32))

        JOBS = [("g", i) for i in range(H1C)] + [("u", i) for i in range(H1C)] + \
               [("d", c) for c in range(DC)]
        job_state = {"done": 0}
        t_job_r = {}
        t_job_v = {}
        t_job_tr = {}
        t_job_sp = {}

        def emit_wjob(j):
            kind, i = JOBS[j]
            buf = j % 2
            src = {"g": w1g_in, "u": w1u_in, "d": w1d_in}[kind]
            ncols = D if kind in ("g", "u") else H1
            nch = DC if kind in ("g", "u") else H1C
            if j == 0:
                pl.wait("gpsimd", tS, t_y_frontier)
            else:
                pl.wait("gpsimd", vS, t_job_r[j - 1])
            t_ld = pl.dma("gpsimd", qWJ, wf_sb[:, buf, 0:ncols], src[P * i:P * (i + 1), :])
            pl.wait("vector", qWJ, t_ld)
            t_job_r[j] = pl.op("vector", vS, lambda E, buf=buf, n=ncols: E.reduce_max(
                wst[:, 0:1], wf_sb[:, buf, 0:n], axis=AX.X, apply_absolute_value=True))
            pl.wait("gpsimd", vS, t_job_r[j])
            t_gsc = pl.op("gpsimd", gS, lambda E: E.tensor_scalar(
                wst[:, 1:2], wst[:, 0:1], INV127, None, ALU.mult))
            pl.wait("vector", gS, t_gsc)
            pl.op("vector", vS, lambda E: E.reciprocal(wst[:, 2:3], wst[:, 1:2]))
            if j >= 1:
                pl.wait("vector", qWT, t_job_tr[j - 1])
            pl.op("vector", vS, lambda E, buf=buf, n=ncols: E.tensor_scalar(
                wq_i32[:, 0:n], wf_sb[:, buf, 0:n], wst[:, 2:3], None, ALU.mult))
            t_q = pl.op("vector", vS, lambda E, buf=buf, n=ncols: E.tensor_scalar(
                wq_nat[:, buf, 0:n], wq_i32[:, 0:n], wst[:, 1:2], None, ALU.mult))
            t_job_v[j] = t_q
            pl.wait("sync", vS, t_q)
            if j == 0:
                pl.wait("sync", qNQ, t_nq)
            if j >= 1:
                pl.wait("sync", gS, j)
                pl.wait("sync", qSP, t_job_sp[j - 1])
            t_tr = pl.dmat("sync", qWT, wq_st[:, 0:nch, :], wq_nat[:, buf, 0:ncols])
            t_job_tr[j] = t_tr
            pl.wait("gpsimd", qWT, t_tr)
            if kind == "g":
                t_sp = pl.dma("gpsimd", qSP, d_wgq[i], wq_st[:, 0:nch, :].rearrange("p c k -> p (c k)"))
            elif kind == "u":
                t_sp = pl.dma("gpsimd", qSP, d_wuq[i], wq_st[:, 0:nch, :].rearrange("p c k -> p (c k)"))
            else:
                t_sp = pl.dma("gpsimd", qSP, d_wdq[:, :, P * i:P * (i + 1)].rearrange("j p k -> p j k"),
                              wq_st[:, 0:nch, :])
            t_job_sp[j] = t_sp
            pl.eng_inc("gpsimd", gS)

        def emit_wjobs(n):
            while n > 0 and job_state["done"] < len(JOBS):
                emit_wjob(job_state["done"])
                job_state["done"] += 1
                n -= 1

        t_x3w = []
        with ExitStack() as ph:
            pe = ph.enter_context
            p_out = [[pe(nc.psum_tensor("p_o%d_%d" % (b, h2), [P, 512], F32)) for h2 in range(2)]
                     for b in range(3)]
            p_sc = pe(nc.psum_tensor("p_sc", [P, 384], F32))
            p_z = pe(nc.psum_tensor("p_z", [P, 16], F32))
            P_sb = pe(nc.sbuf_tensor("P_sb", [P, 2, 384], BF16))
            v_str = pe(nc.sbuf_tensor("v_str", [P, 2, 8, D], BF16))
            rz_sb = pe(nc.sbuf_tensor("rz_sb", [P, 16], F32))
            x3_st = pe(nc.sbuf_tensor("x3_st", [P, 3, D], F32))

            emit_wjobs(8)

            KT = NCORE * TOK // P
            G8 = KT // 8
            pl.wait("sync", cV, t_ag_v)
            pl.wait("tensor", qYL, t_yall)
            pl.wait("tensor", qNQ, t_nq)
            pl.wait("tensor", vS, t_ms)

            t_pv_prev = None
            t_ln3_stats = {}
            for g, (qoff, nqb) in enumerate(QG_GROUPS):
                nq = 128 * nqb
                t_score = {}
                t_pv = {}
                t_vload = {}
                sS_base = sS.v
                if g > 0:
                    pl.wait("tensor", vS, t_x3w[-1][0])

                def mk_score(kt):
                    cblk, off = kt // NT, (kt % NT) * P
                    return pl.op("tensor", tS, lambda E, cblk=cblk, off=off, qoff=qoff, nq=nq: E.matmul(
                        p_sc[:, 0:nq], yT_all[:, cblk, off:off + P], yq_ext[:, qoff:qoff + nq],
                        start=True, stop=True))

                for kt in range(KT):
                    g8, j8, vbuf = kt // 8, kt % 8, (kt // 8) % 2
                    pb = kt % 2
                    if j8 == 0:
                        if g8 >= 1:
                            pl.wait("sync", tS, t_pv[8 * (g8 - 1)])
                        elif g > 0:
                            pl.wait("sync", tS, t_pv_prev[8 * (G8 - 1)])
                        rows = d_vgat.ap()[1024 * g8:1024 * (g8 + 1), :]
                        t_vload[g8] = pl.dma("sync", qVL, v_str[:, vbuf, :, :],
                                             rows.rearrange("(j p) f -> p j f", p=P))
                        if g + g8 > 0:
                            emit_wjobs(1)
                    if kt == 0:
                        t_score[0] = mk_score(0)
                    pl.wait("scalar", tS, t_score[kt])
                    pl.op("scalar", sS, lambda E, pb=pb, nq=nq: E.activation(
                        P_sb[:, pb, 0:nq], p_sc[:, 0:nq], AF.Exp))
                    pl.wait("tensor", sS, sS_base + kt + 1)
                    if kt + 1 < KT:
                        t_score[kt + 1] = mk_score(kt + 1)
                    if j8 == 0:
                        pl.wait("tensor", qVL, t_vload[g8])
                    first, last = (kt == 0), (kt == KT - 1)
                    for b in range(nqb):
                        pl.op("tensor", tS, lambda E, b=b, pb=pb, j8=j8, vbuf=vbuf, first=first, last=last: E.matmul(
                            p_out[b][0][:, :], P_sb[:, pb, P * b:P * (b + 1)],
                            v_str[:, vbuf, j8, 0:512], start=first, stop=last))
                        pl.op("tensor", tS, lambda E, b=b, pb=pb, j8=j8, vbuf=vbuf, first=first, last=last: E.matmul(
                            p_out[b][1][:, :], P_sb[:, pb, P * b:P * (b + 1)],
                            v_str[:, vbuf, j8, 512:1024], start=first, stop=last))
                        t_pv[kt] = pl.op("tensor", tS, lambda E, b=b, pb=pb, first=first, last=last, nqb=nqb: E.matmul(
                            p_z[:, b:b + 1], P_sb[:, pb, P * b:P * (b + 1)], c_ones[:, 0:1],
                            start=(first and b == 0), stop=(last and b == nqb - 1)))
                pl.wait("vector", tS, t_pv[KT - 1])
                for b in range(nqb):
                    m = (qoff // P) + b
                    sb2 = b
                    pl.op("vector", vS, lambda E, b=b: E.reciprocal(rz_sb[:, b:b + 1], p_z[:, b:b + 1]))
                    if t_x3w:
                        pl.wait("vector", qX3W, t_x3w[-1][1])
                    pl.op("vector", vS, lambda E, b=b, m=m, sb2=sb2: E.scalar_tensor_tensor(
                        x3_st[:, sb2, 0:512], p_out[b][0][:, :], rz_sb[:, b:b + 1], xres[:, m, 0:512],
                        ALU.mult, ALU.add))
                    t_ev = pl.op("vector", vS, lambda E, b=b, m=m, sb2=sb2: E.scalar_tensor_tensor(
                        x3_st[:, sb2, 512:1024], p_out[b][1][:, :], rz_sb[:, b:b + 1], xres[:, m, 512:1024],
                        ALU.mult, ALU.add))
                    pl.wait("gpsimd", vS, t_ev)
                    t_w = pl.dma("gpsimd", qX3W, d_x3[P * m:P * (m + 1), :], x3_st[:, sb2, :])
                    t_x3w.append((t_ev, t_w))
                for b in range(nqb):
                    m = (qoff // P) + b
                    sb2 = b
                    pl.op("vector", vS, lambda E, m=m, sb2=sb2: E.reduce_sum(
                        st_sum[:, m:m + 1], x3_st[:, sb2, :], axis=AX.X))
                    pl.op("vector", vS, lambda E, m=m, sb2=sb2: E.reduce_max(
                        mxacc[:, m:m + 1], x3_st[:, sb2, :], axis=AX.X))
                    pl.op("vector", vS, lambda E, m=m, sb2=sb2: E.tensor_reduce(
                        mxacc[:, 16 + m:17 + m], x3_st[:, sb2, :], axis=AX.X,
                        op=ALU.min, negate=True))
                    t_q2 = pl.op("vector", vS, lambda E, sb2=sb2: E.tensor_tensor(
                        sq_f32[:, 0, :], x3_st[:, sb2, :], x3_st[:, sb2, :], ALU.mult))
                    pl.wait("vector", vS, t_q2)
                    t_ln3_stats[m] = pl.op("vector", vS, lambda E, m=m: E.reduce_sum(
                        st_sq[:, m:m + 1], sq_f32[:, 0, :], axis=AX.X))
                t_pv_prev = t_pv
            emit_wjobs(len(JOBS))
            t_attn_tensor = t_pv_prev[KT - 1]

        wprep.close()
        ph3arena.close()
        s123.close()

        with ExitStack() as ph:
            pe = ph.enter_context
            x3_t = pe(nc.sbuf_tensor("x3_t", [P, 2, D], F32))
            mxrow = pe(nc.sbuf_tensor("mxrow", [1, P], F32))
            xq_i32 = pe(nc.sbuf_tensor("xq_i32", [P, 2, D], I32))
            xq_f32 = pe(nc.sbuf_tensor("xq_f32", [P, D], F32))
            h1_fm = pe(nc.sbuf_tensor("h1_fm", [P, H1C, TOK], BF16))
            wg_ch = pe(nc.sbuf_tensor("wg_ch", [P, 2, D], BF16))
            wu_ch = pe(nc.sbuf_tensor("wu_ch", [P, 2, D], BF16))
            silu1 = pe(nc.sbuf_tensor("silu1", [P, 2, 512], BF16))
            hmx = pe(nc.sbuf_tensor("hmx", [P, 96], F32))
            p_g1 = [pe(nc.psum_tensor("p_g1%d" % b, [P, 512], F32)) for b in range(2)]
            p_u1 = [pe(nc.psum_tensor("p_u1%d" % b, [P, 512], F32)) for b in range(2)]
            ln3_t = pe(nc.sbuf_tensor("ln3_t", [P, 2, D], BF16))
            ph4b = ExitStack()
            xq_tm = ph4b.enter_context(nc.sbuf_tensor("xq_tm", [P, 2, D], BF16))
            xq_fm = ph4b.enter_context(nc.sbuf_tensor("xq_fm", [P, NT, DC, P], BF16))

            pl.wait("vector", qSP, t_job_sp[len(JOBS) - 1])
            pl.wait("sync", vS, t_job_v[len(JOBS) - 1])
            pl.wait("sync", qWT, t_job_tr[len(JOBS) - 1])
            pl.wait("sync", qWJ, qWJ.v)
            pl.wait("gpsimd", vS, t_ln3_stats[NT - 1])
            pl.op("gpsimd", gS, lambda E: E.tensor_scalar(
                st_w[:, 0, 0:NT], st_sum[:, 0:NT], 1.0 / D, None, ALU.mult))
            pl.op("gpsimd", gS, lambda E: E.tensor_scalar(
                st_w[:, 1, 0:NT], st_sq[:, 0:NT], 1.0 / D, None, ALU.mult))
            pl.op("gpsimd", gS, lambda E: E.tensor_tensor(
                st_w[:, 2, 0:NT], st_w[:, 0, 0:NT], st_w[:, 0, 0:NT], ALU.mult))
            pl.op("gpsimd", gS, lambda E: E.tensor_tensor(
                st_w[:, 3, 0:NT], st_w[:, 1, 0:NT], st_w[:, 2, 0:NT], ALU.subtract))
            t_gv3 = pl.op("gpsimd", gS, lambda E: E.tensor_scalar(
                st_w[:, 4, 0:NT], st_w[:, 3, 0:NT], EPS, None, ALU.add))
            pl.wait("scalar", gS, t_gv3)
            t_sd3 = pl.op("scalar", sS, lambda E: E.activation(
                st_w[:, 5, 0:NT], st_w[:, 4, 0:NT], AF.Sqrt))
            pl.wait("vector", sS, t_sd3)
            t_rc3 = pl.op("vector", vS, lambda E: E.reciprocal(st_w[:, 6, 0:NT], st_w[:, 5, 0:NT]))
            pl.op("gpsimd", gS, lambda E: E.tensor_tensor(
                st_w[:, 7, 0:NT], mxacc[:, 0:NT], st_w[:, 0, 0:NT], ALU.subtract))
            pl.op("gpsimd", gS, lambda E: E.tensor_tensor(
                st_w[:, 8, 0:NT], mxacc[:, 16:16 + NT], st_w[:, 0, 0:NT], ALU.add))
            t_ab = pl.op("gpsimd", gS, lambda E: E.tensor_tensor(
                st_w[:, 9, 0:NT], st_w[:, 7, 0:NT], st_w[:, 8, 0:NT], ALU.subtract))
            pl.wait("vector", gS, t_ab)
            t_relu = pl.op("vector", vS, lambda E: E.tensor_single_scalar(
                st_w[:, 7, 0:NT], st_w[:, 9, 0:NT], 0.0, ALU.max))
            pl.wait("gpsimd", vS, t_relu)
            pl.op("gpsimd", gS, lambda E: E.tensor_tensor(
                st_w[:, 9, 0:NT], st_w[:, 7, 0:NT], st_w[:, 8, 0:NT], ALU.add))
            pl.wait("gpsimd", vS, t_rc3)
            t_lmx = pl.op("gpsimd", gS, lambda E: E.tensor_tensor(
                st_w[:, 7, 0:NT], st_w[:, 9, 0:NT], st_w[:, 6, 0:NT], ALU.mult))
            pl.wait("vector", gS, t_lmx)
            t_rf = pl.op("vector", vS, lambda E: E.reduce_max(
                mxacc[:, 16:17], st_w[:, 7, 0:NT], axis=AX.X))
            pl.wait("scalar", vS, t_rf)
            pl.wait("scalar", qWT, t_job_tr[len(JOBS) - 1])
            t_mrow = pl.dma("scalar", qMW, mxrow[:, :], mxacc[:, 16:17])
            pl.wait("vector", qMW, t_mrow)
            t_rg = pl.op("vector", vS, lambda E: E.reduce_max(mx_sb[0:1, 0:1], mxrow[0:1, :], axis=AX.X))
            pl.wait("vector", vS, t_rg)
            t_g1 = pl.op("vector", vS, lambda E: E.memset(mx_sb[0:1, 1:8], 0.0))
            pl.wait("gpsimd", vS, t_g1)
            t_mst = pl.dma("gpsimd", qM1a, d_mx1_in.ap(), mx_sb[0:1, 0:8])
            pl.wait("gpsimd", qM1a, t_mst)
            t_ar1 = pl.op("gpsimd", cM1, lambda E: E.collective_compute(
                "AllReduce", ALU.max, replica_groups=[list(range(NCORE))],
                ins=[d_mx1_in.ap().opt()], outs=[d_mx1_out.ap().opt()]))
            pl.wait("gpsimd", cM1, t_ar1)
            t_mld = pl.dma("gpsimd", qM1b, mx_sb[0:1, 0:8], d_mx1_out.ap())
            pl.wait("scalar", qM1b, t_mld)
            pl.wait("scalar", qCN, t_consts)
            t_sg = pl.op("scalar", sS, lambda E: E.activation(scal_sb[0:1, 0:1], sclip[0:1, 0:1], AF.Sigmoid))
            pl.wait("gpsimd", sS, t_sg)
            pl.op("gpsimd", gS, lambda E: E.tensor_tensor(
                scal_sb[0:1, 1:2], scal_sb[0:1, 0:1], mx_sb[0:1, 0:1], ALU.mult))
            t_gsx = pl.op("gpsimd", gS, lambda E: E.tensor_scalar(
                scal_sb[0:1, 2:3], scal_sb[0:1, 1:2], INV127, None, ALU.mult))
            pl.wait("vector", gS, t_gsx)
            t_sx = pl.op("vector", vS, lambda E: E.reciprocal(scal_sb[0:1, 3:4], scal_sb[0:1, 2:3]))
            pl.wait("tensor", vS, t_sx)
            pl.wait("tensor", qCN, t_consts)
            t_bc = pl.op("tensor", tS, lambda E: E.matmul(
                p_g1[0][:, 0:2], c_onesf[:], scal_sb[0:1, 2:4], start=True, stop=True))
            pl.wait("vector", tS, t_bc)
            t_sxb = pl.op("vector", vS, lambda E: E.tensor_copy(sxb[:, 0:2], p_g1[0][:, 0:2]))

            t_xq = {}
            t_xc = 0
            for m in range(NT):
                buf = m % 2
                if m == 0:
                    pl.wait("vector", vS, t_sxb)
                if m >= 2:
                    pl.wait("sync", vS, t_xc)
                pl.wait("sync", qX3W, t_x3w[m][1])
                t_lq = pl.dma("sync", qX3D, x3_t[:, buf, :], d_x3[P * m:P * (m + 1), :])
                if m >= 1:
                    pl.wait("vector", qTX, t_xq[m - 1])
                pl.wait("vector", qX3D, t_lq)
                pl.op("vector", vS, lambda E, m=m, buf=buf: E.tensor_scalar(
                    ln3_t[:, buf, :], x3_t[:, buf, :], st_w[:, 0, m:m + 1], st_w[:, 6, m:m + 1],
                    ALU.subtract, ALU.mult))
                pl.op("vector", vS, lambda E, buf=buf: E.tensor_scalar(
                    xq_f32[:, :], ln3_t[:, buf, :], sxb[:, 1:2], 127.0, ALU.mult, ALU.min))
                pl.op("vector", vS, lambda E: E.tensor_single_scalar(
                    xq_i32[:, 0, :], xq_f32[:, :], -128.0, ALU.max))
                t_c = pl.op("vector", vS, lambda E, buf=buf: E.tensor_copy(
                    xq_tm[:, buf, :], xq_i32[:, 0, :]))
                t_xc = t_c
                pl.wait("sync", vS, t_c)
                t_xq[m] = pl.dmat("sync", qTX, xq_fm[:, m, :, :], xq_tm[:, buf, :])

            t_h1 = {}
            t_humx = {}
            t_gu1_first = {}
            first_silu = True
            for i in range(H1C):
                ib = i % 2
                if i == 0:
                    pl.wait("sync", tS, t_attn_tensor)
                    pl.wait("sync", qSP, t_job_sp[len(JOBS) - 1])
                else:
                    pl.wait("sync", tS, t_gu1_first[i - 1])
                t_g = pl.dma("sync", qWQL, wg_ch[:, ib, :], d_wgq[i])
                t_u = pl.dma("sync", qWQL, wu_ch[:, ib, :], d_wuq[i])
                pl.wait("tensor", qWQL, t_u)
                if i == 0:
                    pl.wait("tensor", vS, t_sxb)
                for tg in range(4):
                    slot = 4 * i + tg
                    pb = slot % 2
                    if i == 0:
                        pl.wait("tensor", qTX, t_xq[4 * tg + 3])
                    if slot >= 2:
                        pl.wait("tensor", vS, t_h1[((slot - 2) // 4, (slot - 2) % 4, "h")])
                    first_of_chunk = (tg == 0)
                    for c in range(DC):
                        t_mmg = pl.op("tensor", tS, lambda E, tg=tg, c=c, ib=ib, pb=pb: E.matmul(
                            p_g1[pb][:, :], wg_ch[:, ib, P * c:P * (c + 1)],
                            xq_fm[:, 4 * tg:4 * tg + 4, c, :], start=(c == 0), stop=(c == DC - 1)))
                        if first_of_chunk:
                            t_gu1_first[i] = t_mmg
                            first_of_chunk = False
                    for c in range(DC):
                        t_mmu = pl.op("tensor", tS, lambda E, tg=tg, c=c, ib=ib, pb=pb: E.matmul(
                            p_u1[pb][:, :], wu_ch[:, ib, P * c:P * (c + 1)],
                            xq_fm[:, 4 * tg:4 * tg + 4, c, :], start=(c == 0), stop=(c == DC - 1)))
                    t_h1[(i, tg, "mm")] = t_mmu
                    pl.wait("scalar", tS, t_mmu)
                    if first_silu:
                        pl.wait("scalar", vS, t_sxb)
                        first_silu = False
                    t_si = pl.op("scalar", sS, lambda E, pb=pb: E.activation(
                        silu1[:, pb, :], p_g1[pb][:, :], AF.Silu, scale=sxb[:, 0:1]))
                    pl.wait("vector", sS, t_si)
                    t_h = pl.op("vector", vS, lambda E, i=i, tg=tg, pb=pb: E.tensor_tensor(
                        h1_fm[:, i, 512 * tg:512 * (tg + 1)], silu1[:, pb, :], p_u1[pb][:, :], ALU.mult))
                    t_h1[(i, tg, "h")] = t_h
                    pl.wait("vector", vS, t_h)
                    t_humx[(i, tg)] = pl.op("vector", vS, lambda E, i=i, tg=tg: E.reduce_max(
                        hmx[:, 4 * i + tg:4 * i + tg + 1], h1_fm[:, i, 512 * tg:512 * (tg + 1)],
                        axis=AX.X, apply_absolute_value=True))

            pl.wait("vector", vS, t_humx[(H1C - 1, 3)])
            t_rhf = pl.op("vector", vS, lambda E: E.reduce_max(
                hmx[:, 88:89], hmx[:, 0:4 * H1C], axis=AX.X))
            pl.wait("scalar", vS, t_rhf)
            t_hrow = pl.dma("scalar", qHW, mxrow[:, :], hmx[:, 88:89])
            pl.wait("vector", qHW, t_hrow)
            t_rg2 = pl.op("vector", vS, lambda E: E.reduce_max(mxh_sb[0:1, 0:1], mxrow[0:1, :], axis=AX.X))
            pl.wait("vector", vS, t_rg2)
            t_g2 = pl.op("vector", vS, lambda E: E.memset(mxh_sb[0:1, 1:8], 0.0))
            pl.wait("gpsimd", vS, t_g2)
            t_hst = pl.dma("gpsimd", qM2a, d_mx2_in.ap(), mxh_sb[0:1, 0:8])
            pl.wait("gpsimd", qM2a, t_hst)
            t_ar2 = pl.op("gpsimd", cM2, lambda E: E.collective_compute(
                "AllReduce", ALU.max, replica_groups=[list(range(NCORE))],
                ins=[d_mx2_in.ap().opt()], outs=[d_mx2_out.ap().opt()]))
            pl.wait("gpsimd", cM2, t_ar2)
            t_hld = pl.dma("gpsimd", qM2b, mxh_sb[0:1, 0:8], d_mx2_out.ap())
            pl.wait("scalar", qM2b, t_hld)
            t_sg2 = pl.op("scalar", sS, lambda E: E.activation(scal_sb[0:1, 4:5], sclip[0:1, 1:2], AF.Sigmoid))
            pl.wait("gpsimd", sS, t_sg2)
            pl.op("gpsimd", gS, lambda E: E.tensor_tensor(
                scal_sb[0:1, 5:6], mxh_sb[0:1, 0:1], scal_sb[0:1, 2:3], ALU.mult))
            pl.op("gpsimd", gS, lambda E: E.tensor_tensor(
                scal_sb[0:1, 6:7], scal_sb[0:1, 5:6], scal_sb[0:1, 4:5], ALU.mult))
            t_gsh = pl.op("gpsimd", gS, lambda E: E.tensor_scalar(
                scal_sb[0:1, 7:8], scal_sb[0:1, 6:7], INV127, None, ALU.mult))
            pl.wait("vector", gS, t_gsh)
            t_rsh = pl.op("vector", vS, lambda E: E.reciprocal(scal_sb[0:1, 9:10], scal_sb[0:1, 7:8]))
            pl.wait("gpsimd", vS, t_rsh)
            t_fq = pl.op("gpsimd", gS, lambda E: E.tensor_tensor(
                scal_sb[0:1, 8:9], scal_sb[0:1, 2:3], scal_sb[0:1, 9:10], ALU.mult))
            pl.wait("tensor", gS, t_fq)
            t_bc2 = pl.op("tensor", tS, lambda E: E.matmul(
                p_g1[0][:, 2:4], c_onesf[:], scal_sb[0:1, 7:9], start=True, stop=True))
            pl.wait("vector", tS, t_bc2)
            t_shb = pl.op("vector", vS, lambda E: E.tensor_copy(shb[:, 0:2], p_g1[0][:, 2:4]))

            t_hq = 0
            hq_sS = []
            t_hq_half = {}
            for hh in range(2):
                for i in range(H1C):
                    cnt = H1C * hh + i
                    ib2 = cnt % 2
                    if cnt == 0:
                        pl.wait("vector", vS, t_shb)
                    if cnt >= 2:
                        pl.wait("vector", sS, hq_sS[cnt - 2])
                    pl.op("vector", vS, lambda E, i=i, hh=hh: E.tensor_scalar(
                        xq_f32[:, :], h1_fm[:, i, 1024 * hh:1024 * (hh + 1)], shb[:, 1:2], 127.0,
                        ALU.mult, ALU.min))
                    t_cv = pl.op("vector", vS, lambda E, ib2=ib2: E.tensor_single_scalar(
                        xq_i32[:, ib2, :], xq_f32[:, :], -128.0, ALU.max))
                    pl.wait("scalar", vS, t_cv)
                    t_hq = pl.op("scalar", sS, lambda E, i=i, hh=hh, ib2=ib2: E.copy(
                        h1_fm[:, i, 1024 * hh:1024 * (hh + 1)], xq_i32[:, ib2, :]))
                    hq_sS.append(t_hq)
                t_hq_half[hh] = t_hq
            ph4b.close()
            wd_sb = pe(nc.sbuf_tensor("wd_sb", [P, H1C, D], BF16))
            y_st = pe(nc.sbuf_tensor("y_st", [P, 2, D], F32))
            p_d1 = [pe(nc.psum_tensor("p_d1%d" % b, [P, 512], F32)) for b in range(4)]
            t_wd = 0
            for j in range(H1C):
                if j == 0:
                    pl.wait("sync", qSP, t_job_sp[len(JOBS) - 1])
                    pl.wait("sync", tS, t_h1[(H1C - 1, 3, "mm")])
                t_wd = pl.dma("sync", qWDL, wd_sb[:, j, :], d_wdq[j])
            t_yw = []
            for m in range(NT):
                buf = m % 2
                pb0, pb1 = 2 * (m % 2), 2 * (m % 2) + 1
                if m >= 1:
                    pl.wait("sync", vS, t_yw[m - 1][0])
                t_l = pl.dma("sync", qX3D, x3_t[:, buf, :], d_x3[P * m:P * (m + 1), :])
                if m >= 2:
                    pl.wait("tensor", vS, t_yw[m - 2][0])
                for j in range(H1C):
                    if m == 0 and j == 0:
                        pl.wait("tensor", sS, t_hq_half[0])
                        pl.wait("tensor", qWDL, t_wd)
                    if m == 8 and j == 0:
                        pl.wait("tensor", sS, t_hq_half[1])
                    pl.op("tensor", tS, lambda E, j=j, m=m, pb0=pb0: E.matmul(
                        p_d1[pb0][:, :], h1_fm[:, j, P * m:P * (m + 1)], wd_sb[:, j, 0:512],
                        start=(j == 0), stop=(j == H1C - 1)))
                    t_dm = pl.op("tensor", tS, lambda E, j=j, m=m, pb1=pb1: E.matmul(
                        p_d1[pb1][:, :], h1_fm[:, j, P * m:P * (m + 1)], wd_sb[:, j, 512:1024],
                        start=(j == 0), stop=(j == H1C - 1)))
                pl.wait("vector", tS, t_dm)
                pl.wait("vector", qX3D, t_l)
                if m >= 1:
                    pl.wait("vector", qYO, t_yw[m - 1][1])
                pl.op("vector", vS, lambda E, pb0=pb0, buf=buf: E.scalar_tensor_tensor(
                    y_st[:, buf, 0:512], p_d1[pb0][:, :], shb[:, 0:1], x3_t[:, buf, 0:512],
                    ALU.mult, ALU.add))
                t_ev = pl.op("vector", vS, lambda E, pb1=pb1, buf=buf: E.scalar_tensor_tensor(
                    y_st[:, buf, 512:1024], p_d1[pb1][:, :], shb[:, 0:1], x3_t[:, buf, 512:1024],
                    ALU.mult, ALU.add))
                pl.wait("gpsimd", vS, t_ev)
                t_yw.append((t_ev, pl.dma("gpsimd", qYO, y_out[P * m:P * (m + 1), :], y_st[:, buf, :])))
            if dump:
                pl.dma("gpsimd", qDP, tp_v, d_v_in.ap())
                pl.dma("gpsimd", qDP, tp_y, d_y_in.ap())
                pl.dma("gpsimd", qDP, tp_x3, d_x3)
                pl.dma("gpsimd", qDP, tp_sc[0:1, 0:8], scal_sb[0:1, 0:8])
                t_dump = pl.dma("gpsimd", qDP, tp_sc[0:1, 8:16], mx_sb[0:1, 0:8])
                pl.wait("gpsimd", qDP, t_dump)
            pl.wait("gpsimd", qYO, t_yw[-1][1])

        pl.check()

        block = ent(nc.Block())
        block.sync(lambda E: [f(E) for f in pl.ops["sync"]])
        block.scalar(lambda E: [f(E) for f in pl.ops["scalar"]])
        block.vector(lambda E: [f(E) for f in pl.ops["vector"]])
        block.tensor(lambda E: [f(E) for f in pl.ops["tensor"]])
        block.gpsimd(lambda E: [f(E) for f in pl.ops["gpsimd"]])

    return nc


def _host_inputs(L0_attn_x, L0_ffn_w_gate, L0_ffn_w_up, L0_ffn_w_down, L1_attn_U,
                 L1_ffn_clip_x, L1_ffn_clip_down, L1_ffn_w_gate, L1_ffn_w_up, L1_ffn_w_down):
    bf = ml_dtypes.bfloat16
    x = np.ascontiguousarray(np.asarray(L0_attn_x, dtype=np.float32).reshape(NCORE * TOK, D))

    def pack_lhsT(wT, hc):
        return np.ascontiguousarray(
            wT.reshape(DC, P, hc, P).transpose(2, 1, 0, 3).reshape(hc, P, DC * P))

    w0g_p = pack_lhsT(np.asarray(L0_ffn_w_gate, dtype=np.int32).T, H0C)
    w0u_p = pack_lhsT(np.asarray(L0_ffn_w_up, dtype=np.int32).T, H0C)
    w0d_p = np.ascontiguousarray(np.asarray(L0_ffn_w_down, dtype=np.int32).T.reshape(H0C, P, D))
    w1g = np.ascontiguousarray(np.asarray(L1_ffn_w_gate, dtype=np.float32))
    w1u = np.ascontiguousarray(np.asarray(L1_ffn_w_up, dtype=np.float32))
    w1d = np.ascontiguousarray(np.asarray(L1_ffn_w_down, dtype=np.float32))
    U = np.ascontiguousarray(np.asarray(L1_attn_U, dtype=np.float32).reshape(DC, P, R))
    clip2 = np.stack([np.asarray(L1_ffn_clip_x, dtype=np.float32).reshape(()),
                      np.asarray(L1_ffn_clip_down, dtype=np.float32).reshape(())]).reshape(1, 2)

    trilT = np.triu(np.ones((P, P), dtype=np.float32)).astype(bf)
    sel16 = np.zeros((P, NT, NT), dtype=np.float32)
    for k in range(NT):
        sel16[:, k, k] = 1.0
    sel16 = sel16.reshape(P, NT * NT).astype(bf)
    maskLT = (np.arange(NT)[:, None] < np.arange(NT)[None, :]).astype(np.float32).astype(bf)
    onespack = np.ones((P, P), dtype=np.float32).astype(bf)
    onesf = np.ones((1, P), dtype=np.float32)

    in_maps = []
    for c in range(NCORE):
        cidm = np.broadcast_to((np.arange(NCORE)[:, None] < c), (NCORE, 32)).astype(np.float32).astype(bf)
        tglob = TOK * c + P * np.arange(NT)[None, :] + np.arange(P)[:, None]
        invcnt = (1.0 / (tglob + 1.0)).astype(np.float32)
        in_maps.append({
            "x": x[TOK * c:TOK * (c + 1)],
            "w0g": w0g_p, "w0u": w0u_p, "w0d": w0d_p,
            "w1g": w1g, "w1u": w1u, "w1d": w1d,
            "uproj": U, "clip2": clip2,
            "trilT": trilT, "sel16": sel16, "maskLT": maskLT,
            "cidm": cidm, "invcnt": invcnt,
            "onespack": onespack, "onesf": onesf,
        })
    return in_maps


def kernel(L0_attn_x, L0_ffn_w_gate, L0_ffn_w_up, L0_ffn_w_down, L1_attn_U,
           L1_ffn_clip_x, L1_ffn_clip_down, L1_ffn_w_gate, L1_ffn_w_up, L1_ffn_w_down):
    in_maps = _host_inputs(L0_attn_x, L0_ffn_w_gate, L0_ffn_w_up, L0_ffn_w_down, L1_attn_U,
                           L1_ffn_clip_x, L1_ffn_clip_down, L1_ffn_w_gate, L1_ffn_w_up,
                           L1_ffn_w_down)
    nc = build_program()
    res = run_bass_kernel_spmd(nc, in_maps, core_ids=list(range(NCORE)))
    out = np.concatenate([np.asarray(r["y"]) for r in res.results], axis=0)
    return out.reshape(1, NCORE * TOK, D).astype(np.float32)
